# version 08 (final)
# baseline (speedup 1.0000x reference)
import numpy as np
import ml_dtypes
import concourse.bass as bass
import concourse.mybir as mybir
from concourse.bass_utils import run_bass_kernel_spmd

F32 = mybir.dt.float32
BF16 = mybir.dt.bfloat16
I32 = mybir.dt.int32
AF = mybir.ActivationFunctionType
ALU = mybir.AluOpType
AX = mybir.AxisListType

B, T, C, H = 16, 2048, 1024, 16
II = 2816
NB = 2
NG = T // 512
GT = 4
FC = C // 128
IC = II // 128
QLO = -128.49999
QHI = 127.49999

ENGINES = ("T", "V", "S", "G", "Y")
LOAD_SEMS = ("LA", "LB", "LW0", "LW1", "LP0", "LP1", "LP2", "LBK")
STORE_SEMS = ("SA", "SB", "SC", "SD", "SH", "SX", "SM0", "SM1", "SM2",
              "SO0", "SO1", "SBK")
DMA_SEMS = LOAD_SEMS + STORE_SEMS
BUMP_SEMS = {"LBK", "SBK"}
SEMS = ("T", "V", "S", "G", "CC") + DMA_SEMS


class Prog:

    def __init__(self):
        self.ops = {e: [] for e in ENGINES}
        self.cnt = dict.fromkeys(SEMS, 0)
        self.small_ev = {"V": set(), "S": set()}
        self.small_win = {"V": 0, "S": 0}

    def op(self, eng, fn, waits=(), inc=None, n=1):
        w = []
        for x in waits:
            if x is None:
                continue
            s, v = x
            if s in BUMP_SEMS:
                v = self.cnt[s]
            if v and v > 0:
                w.append((s, v))
        self.ops[eng].append((fn, tuple(w), inc, n))
        if inc is not None:
            self.cnt[inc] += n
            return (inc, self.cnt[inc])
        return None

    def t(self, fn, waits=()):
        return self.op("T", fn, waits, "T")

    def _vs(self, eng, fn, waits, small):
        w = list(waits)
        need = self.small_win[eng] > 0
        if not need:
            for x in w:
                if x is not None and x[0] == eng and x[1] in self.small_ev[eng]:
                    need = True
                    break
        if need:
            w.append((eng, self.cnt[eng]))
        if self.small_win[eng] > 0:
            self.small_win[eng] -= 1
        ev = self.op(eng, fn, w, eng)
        if small:
            self.small_ev[eng].add(ev[1])
            self.small_win[eng] = 2
        return ev

    def v(self, fn, waits=(), small=False):
        return self._vs("V", fn, waits, small)

    def s(self, fn, waits=(), small=False):
        return self._vs("S", fn, waits, small)

    def g(self, fn, waits=()):
        w = list(waits) + [("G", self.cnt["G"])]
        return self.op("G", fn, w, "G")

    def dy(self, fn, waits=(), slot="LBK"):
        w = list(waits)
        if slot in BUMP_SEMS:
            w.append((slot, self.cnt[slot]))
        return self.op("Y", fn, w, slot, 16)

    def dg(self, fn, waits=(), slot="SBK"):
        w = list(waits)
        if slot in BUMP_SEMS:
            w.append((slot, self.cnt[slot]))
        return self.op("G", fn, w, slot, 16)

    def cc(self, fn, waits=()):
        return self.op("G", fn, waits, "CC")

    def rendezvous(self):
        allw = [(s, self.cnt[s]) for s in ("T", "V", "S", "G") + DMA_SEMS]
        for e in ENGINES:
            self.op(e, None, allw)
        return allw

    def replay(self, ename, eng, sems):
        hw = dict.fromkeys(SEMS, 0)
        own = {"T": "T", "V": "V", "S": "S", "G": "G", "Y": None}[ename]
        skip_own = ename in ("T", "Y")
        for fn, waits, inc, n in self.ops[ename]:
            for s, v in waits:
                if s == own and skip_own:
                    continue
                if v > hw[s]:
                    eng.wait_ge(sems[s], v)
                    hw[s] = v
            if fn is None:
                continue
            inst = fn(eng)
            if inc is not None:
                inst.then_inc(sems[inc], n)


class Reg:

    def __init__(self):
        self.readers = []
        self.writer = None

    def wbegin(self):
        w = list(self.readers)
        self.readers = []
        return w

    def wend(self, ev):
        self.writer = ev

    def r(self, ev_reader_returns=None):
        return [self.writer] if self.writer else []

    def radd(self, ev):
        self.readers.append(ev)


def build_program(dbg=(), phases=("p0", "p1", "p2"), racecheck=False):
    nc = bass.Bass("TRN2", target_bir_lowering=False,
                   detect_race_conditions=racecheck)

    def dram(name, shape, dt):
        if name in dbg:
            return nc.dram_tensor(name, shape, dt, kind="ExternalOutput").ap()
        return nc.dram_tensor(name, shape, dt).ap()

    d_x = nc.dram_tensor("x", [NB * T, C], F32, kind="ExternalInput").ap()
    d_wqkv = nc.dram_tensor("wqkv", [C, 3 * C], F32, kind="ExternalInput").ap()
    d_wout = nc.dram_tensor("wout", [C, C], F32, kind="ExternalInput").ap()
    d_hs = nc.dram_tensor("hs", [H, 1], F32, kind="ExternalInput").ap()
    d_dirs = nc.dram_tensor("dirs", [H, 8], F32, kind="ExternalInput").ap()
    d_clipx = nc.dram_tensor("clipx", [1, 1], F32, kind="ExternalInput").ap()
    d_clipd = nc.dram_tensor("clipd", [1, 1], F32, kind="ExternalInput").ap()
    d_wg = nc.dram_tensor("wg", [II, C], F32, kind="ExternalInput").ap()
    d_wu = nc.dram_tensor("wu", [II, C], F32, kind="ExternalInput").ap()
    d_wd = nc.dram_tensor("wd", [C, II], F32, kind="ExternalInput").ap()
    d_c = nc.dram_tensor("cvec", [NB, C], F32, kind="ExternalInput").ap()
    d_wmod = nc.dram_tensor("wmod", [C, 6 * C], F32, kind="ExternalInput").ap()
    d_out = nc.dram_tensor("out", [NB * T, C], F32, kind="ExternalOutput").ap()

    d_wqkvs = dram("wqkvs", [C, 3 * C], BF16)
    d_wouts = dram("wouts", [C, C], BF16)
    d_wgs = dram("wgs", [C, II], BF16)
    d_wus = dram("wus", [C, II], BF16)
    d_wds = dram("wds", [II, C], BF16)
    d_x1 = dram("x1s", [NB * T, C], F32)
    d_ht = dram("hts", [C, NB * T], BF16)
    d_hm = dram("hms", [II, NB * T], BF16)
    d_dbgA = dram("dbgA", [128, 512], F32)
    d_dbgB = dram("dbgB", [128, 512], F32)
    d_dbgC = dram("dbgC", [128, 512], F32)
    d_dbgD = dram("dbgD", [128, 4096], F32)
    d_dbgE = dram("dbgE", [128, 96], F32)
    cc1i = nc.dram_tensor("cc1i", [128, 1], F32)
    cc1o = nc.dram_tensor("cc1o", [128, 1], F32)
    cc2i = nc.dram_tensor("cc2i", [128, 1], F32)
    cc2o = nc.dram_tensor("cc2o", [128, 1], F32)

    from contextlib import ExitStack
    ctx = ExitStack()
    en = ctx.enter_context

    ident_f = en(nc.sbuf_tensor("ident_f", [128, 128], F32))
    ones_b = en(nc.sbuf_tensor("ones_b", [128, 128], BF16))
    ones_f = en(nc.sbuf_tensor("ones_f", [128, 128], F32))
    cmask = en(nc.sbuf_tensor("cmask", [128, 128], F32))
    dirsb = en(nc.sbuf_tensor("dirsb", [H, 8], F32))
    hsb = en(nc.sbuf_tensor("hsb", [H, 1], F32))
    hsdir8 = en(nc.sbuf_tensor("hsdir8", [H, 8], F32))
    dirrow = en(nc.sbuf_tensor("dirrow", [1, 128], F32))
    hsrow = en(nc.sbuf_tensor("hsrow", [1, 128], F32))
    dbc = en(nc.sbuf_tensor("dbc", [128, 128], F32))
    hsbc = en(nc.sbuf_tensor("hsbc", [128, 128], F32))
    sigx = en(nc.sbuf_tensor("sigx", [1, 2], F32))
    modT = en(nc.sbuf_tensor("modT", [128, 64], F32))
    caps = en(nc.sbuf_tensor("caps", [128, 32], F32))
    gabc = en(nc.sbuf_tensor("gabc", [128, NB * C], F32))
    gbbc = en(nc.sbuf_tensor("gbbc", [128, NB * C], F32))
    mrow = en(nc.sbuf_tensor("mrow", [1, 512], F32))
    swg = en(nc.sbuf_tensor("swg", [128, IC], F32))
    swu = en(nc.sbuf_tensor("swu", [128, IC], F32))
    sxswg = en(nc.sbuf_tensor("sxswg", [128, IC], F32))
    sxswu = en(nc.sbuf_tensor("sxswu", [128, IC], F32))
    swd1 = en(nc.sbuf_tensor("swd1", [128, 1], F32))
    hmax1 = en(nc.sbuf_tensor("hmax1", [128, 1], F32))
    hmax2 = en(nc.sbuf_tensor("hmax2", [128, 1], F32))
    gmt = en(nc.sbuf_tensor("gmt", [128, 1], F32))
    scal = en(nc.sbuf_tensor("scal", [1, 8], F32))
    sxbc = en(nc.sbuf_tensor("sxbc", [128, 2], F32))
    sdbc = en(nc.sbuf_tensor("sdbc", [128, 2], F32))
    st_s1 = en(nc.sbuf_tensor("st_s1", [128, 8], F32))
    st_s2 = en(nc.sbuf_tensor("st_s2", [128, 8], F32))
    st_m = en(nc.sbuf_tensor("st_m", [128, 8], F32))
    st_v = en(nc.sbuf_tensor("st_v", [128, 8], F32))
    st_r = en(nc.sbuf_tensor("st_r", [128, 8], F32))
    st_n = en(nc.sbuf_tensor("st_n", [128, 8], F32))
    st_x = en(nc.sbuf_tensor("st_x", [128, 2], F32))
    epsb = en(nc.sbuf_tensor("epsb", [128, 1], F32))
    lnzb = en(nc.sbuf_tensor("lnzb", [128, 1024], F32))
    sq_t = en(nc.sbuf_tensor("sq_t", [128, 2 * C], F32))

    ar_b = en(nc.sbuf_tensor("ar_b", [128, 60700], BF16))
    ar_f = en(nc.sbuf_tensor("ar_f", [128, 12850], F32))
    xqi = en(nc.sbuf_tensor("xqi", [128, 512], I32))

    psS = en(nc.psum_tensor("psS", [128, 2048], F32))
    psO = en(nc.psum_tensor("psO", [128, 1024], F32))
    psX = en(nc.psum_tensor("psX", [128, 512], F32))
    psY = en(nc.psum_tensor("psY", [128, 512], F32))

    sems = {s: en(nc.semaphore("sem_" + s)) for s in SEMS}
    pr = Prog()

    RS = [Reg() for _ in range(4)]
    RO = [Reg() for _ in range(2)]
    RX = Reg()
    RY = Reg()

    def LD(dst, src, waits=(), slot="LBK"):
        return pr.dy(lambda e, d=dst, s=src: e.dma_start(out=d, in_=s), waits, slot)

    def ST(dst, src, waits=(), slot="SBK"):
        return pr.dy(lambda e, d=dst, s=src: e.dma_start(out=d, in_=s), waits, slot)

    def carve(n, *sizes):
        out, off = [], 0
        for s in sizes:
            out.append(off)
            off += s
        assert off <= n, (off, n)
        return out

    (o_wqb, o_kT, o_va, o_qT, o_hT, o_pb, o_od, o_rzb, o_ht2) = carve(
        60700, 8192, 16384, 16640, 4096, 4096, 2048, 4096, 1024, 4096)
    (o_wgT, o_wuT, o_htb, o_xq, o_hm, o_sil) = carve(
        60700, 22528, 22528, 8192, 4096, 1536, 512)
    (o_wdT, o_hmg, o_h2i) = carve(60700, 22528, 2 * 11264, 11264)
    (o_xsb, o_xn, o_tev) = carve(12850, 8192, 4096, 512)
    (o_x1b, o_outs) = carve(12850, 2048, 2048)
    (o_wmA, o_wmB, o_crow, o_scT, o_hsdT, o_big, o_pad_, o_big2,
     o_gthr, o_gthrT, o_augs, o_rsw, o_swt, o_swt2, o_rowt) = carve(
        12850, 512, 512, 1024, 16, 16, 3072, 160, 3072, 128, 128, 16, 1, 1, 1, 2048)
    rowt = ar_f[0:1, o_rowt:o_rowt + NB * C]
    (o_cb, o_stg, o_stg2) = carve(60700, 3072, 1024, 1024)

    def AB(off, ln, p0=0, p1=128):
        return ar_b[p0:p1, off:off + ln]

    def AFp(off, ln, p0=0, p1=128):
        return ar_f[p0:p1, off:off + ln]

    pr.g(lambda e: e.memset(ident_f[:, :], 0.0))
    pr.g(lambda e: e.affine_select(out=ident_f[:, :], in_=ident_f[:, :],
                                   compare_op=ALU.not_equal, fill=1.0,
                                   base=0, pattern=[[-1, 128]], channel_multiplier=1))
    pr.g(lambda e: e.memset(ones_b[:, :], 1.0))
    pr.g(lambda e: e.memset(ones_f[:, :], 1.0))
    pr.g(lambda e: e.memset(cmask[:, :], 0.0))
    pr.g(lambda e: e.affine_select(out=cmask[:, :], in_=cmask[:, :],
                                   compare_op=ALU.is_ge, fill=-1.0e9,
                                   base=0, pattern=[[1, 128]], channel_multiplier=-1))
    pr.g(lambda e: e.memset(epsb[:, :], 1e-5))
    pr.g(lambda e: e.memset(hmax1[:, :], 0.0))
    pr.g(lambda e: e.memset(hmax2[:, :], 0.0))
    ev_gc = ("G", pr.cnt["G"])

    e_hs = LD(hsb[:, :], d_hs)
    e_dirs = LD(dirsb[:, :], d_dirs)
    LD(sigx[0:1, 0:1], d_clipx)
    e_clip2 = LD(sigx[0:1, 1:2], d_clipd)
    pr.s(lambda e: e.activation(sigx[0:1, 0:2], sigx[0:1, 0:2], AF.Sigmoid),
         [e_clip2], small=True)
    ev_hsd = pr.v(lambda e: e.tensor_scalar(hsdir8[:, :], dirsb[:, :], hsb[:, 0:1],
                                            8.0, ALU.mult, ALU.mult),
                  [e_dirs, e_hs], small=True)
    ev_dr = pr.dg(lambda e: e.dma_start(
        out=dirrow[0:1, :].rearrange("a (h d) -> a h d", d=8),
        in_=dirsb[:, :]), [e_dirs])
    ev_hr = pr.dg(lambda e: e.dma_start(
        out=hsrow[0:1, :].rearrange("a (h d) -> a h d", d=8),
        in_=hsdir8[:, :]), [ev_hsd])
    ev_dbcm = pr.t(lambda e: e.matmul(psX[:, 0:128], ones_f[0:1, :], dirrow[0:1, :],
                                      start=True, stop=True),
                   [ev_dr, ev_gc] + RX.wbegin())
    RX.wend(ev_dbcm)
    ev_dbc = pr.v(lambda e: e.tensor_copy(dbc[:, :], psX[:, 0:128]), RX.r() + [ev_dbcm])
    RX.radd(ev_dbc)
    ev_hbcm = pr.t(lambda e: e.matmul(psY[:, 0:128], ones_f[0:1, :], hsrow[0:1, :],
                                      start=True, stop=True),
                   [ev_hr, ev_gc] + RY.wbegin())
    RY.wend(ev_hbcm)
    ev_hbc = pr.v(lambda e: e.tensor_copy(hsbc[:, :], psY[:, 0:128]), RY.r() + [ev_hbcm])
    RY.radd(ev_hbc)

    if "dbgA" in dbg:
        pr.dy(lambda e: e.dma_start(out=d_dbgA[:, 288:416], in_=ident_f[:, :]),
              [ev_gc, ("G", pr.cnt["G"])])
    crow = AFp(o_crow, C, 0, NB)
    scT = AFp(o_scT, 16)
    skip_mod = "nomod" in dbg
    skip_wqkv = "nowqkv" in dbg
    e_c = LD(crow, d_c)
    ev_silu = pr.s(lambda e: e.activation(crow, crow, AF.Silu), [e_c])
    ev_sct = None
    for ccx in range(FC if not skip_mod else 0):
        evt = pr.t(lambda e, ccx=ccx: e.transpose(
            psO[0:128, 0:2], crow[0:NB, 128 * ccx:128 * ccx + 128], ident_f[0:NB, 0:NB]),
            [ev_silu, ev_gc] + RO[0].wbegin())
        RO[0].wend(evt)
        ev_sct = pr.v(lambda e, ccx=ccx: e.tensor_copy(scT[:, 2 * ccx:2 * ccx + 2],
                                                       psO[:, 0:2]),
                      RO[0].r() + [evt])
        RO[0].radd(ev_sct)

    wmb = [AFp(o_wmA, 512), AFp(o_wmB, 512)]
    ev_wm_use = [None, None]
    ldseq = 0
    g_mrow = []
    g_rowt = []
    for nch in range(12 if not skip_mod else 0):
        evmm = [None, None]
        wx = RX.wbegin() + RY.wbegin()
        for ccx in range(FC):
            pi = ldseq % 2
            ldseq += 1
            w = [ev_sct]
            if ev_wm_use[pi] is not None:
                w.append(ev_wm_use[pi])
            evld = LD(wmb[pi], d_wmod[128 * ccx:128 * ccx + 128,
                                      512 * nch:512 * nch + 512], w,
                      slot="LA" if pi == 0 else "LB")
            for b in range(NB):
                ps = psX if b == 0 else psY
                evmm[b] = pr.t(lambda e, ccx=ccx, b=b, pi=pi, ps=ps: e.matmul(
                    ps[0:1, 0:512], scT[:, 2 * ccx + b:2 * ccx + b + 1], wmb[pi],
                    start=(ccx == 0), stop=(ccx == FC - 1)),
                    [evld] + (wx if ccx == 0 else []))
            ev_wm_use[pi] = evmm[1]
        RX.wend(evmm[0])
        RY.wend(evmm[1])
        gidx, half = nch // 2, nch % 2
        for b in range(NB):
            ps = psX if b == 0 else psY
            RR = RX if b == 0 else RY
            ev_row = pr.v(lambda e, ps=ps: e.tensor_copy(mrow[0:1, :], ps[0:1, 0:512]),
                          RR.r() + [evmm[b]] + g_mrow, small=True)
            g_mrow = []
            RR.radd(ev_row)
            if gidx in (0, 1, 3, 4):
                gslot = {0: 0, 1: 1, 3: 2, 4: 3}[gidx]
                for q in range(4):
                    col = 32 * b + gslot * 8 + half * 4 + q
                    evt = pr.t(lambda e, q=q: e.transpose(
                        psO[0:128, 512:513], mrow[0:1, 128 * q:128 * q + 128],
                        ident_f[0:1, 0:1]), [ev_row] + RO[1].wbegin())
                    RO[1].wend(evt)
                    g_mrow.append(evt)
                    evc = pr.v(lambda e, col=col: e.tensor_copy(
                        modT[:, col:col + 1], psO[:, 512:513]), RO[1].r() + [evt],
                        small=True)
                    RO[1].radd(evc)
            else:
                evcp = pr.v(lambda e, half=half, b=b: e.tensor_copy(
                    rowt[0:1, C * b + 512 * half:C * b + 512 * half + 512],
                    mrow[0:1, :]),
                    [ev_row] + g_rowt, small=True)
                g_rowt = []
                g_mrow.append(evcp)
                if half == 1:
                    rb0, rb1 = C * b, C * b + 512
                    ev_a = pr.s(lambda e, rb0=rb0: e.activation(
                        sq_t[0:1, 0:C], rowt[0:1, rb0:rb0 + C], AF.Abs), [evcp], small=True)
                    ev_m = pr.v(lambda e: e.tensor_scalar_max(
                        sq_t[0:1, 0:C], sq_t[0:1, 0:C], 0.25), [ev_a], small=True)
                    ev_sg = pr.s(lambda e, rb0=rb0: e.sign(
                        mrow[0:1, :], rowt[0:1, rb0:rb0 + 512]), [ev_m], small=True)
                    g_mrow.append(ev_sg)
                    ev_sg2 = pr.s(lambda e, rb0=rb0, rb1=rb1: e.sign(
                        rowt[0:1, rb0:rb0 + 512], rowt[0:1, rb1:rb1 + 512]), [ev_sg], small=True)
                    ev_r1 = pr.v(lambda e, rb0=rb0, rb1=rb1: e.tensor_mul(
                        rowt[0:1, rb1:rb1 + 512], sq_t[0:1, 512:C],
                        rowt[0:1, rb0:rb0 + 512]), [ev_sg2, ev_m], small=True)
                    ev_r0 = pr.v(lambda e, rb0=rb0: e.tensor_mul(
                        rowt[0:1, rb0:rb0 + 512], sq_t[0:1, 0:512], mrow[0:1, :]),
                        [ev_sg, ev_r1, ev_m], small=True)
                    g_mrow.append(ev_r0)
                    dstbc = gabc if gidx == 2 else gbbc
                    for nh in range(2):
                        evb = pr.t(lambda e, nh=nh, rb0=rb0: e.matmul(
                            psO[:, 0:512], ones_f[0:1, :],
                            rowt[0:1, rb0 + 512 * nh:rb0 + 512 * nh + 512],
                            start=True, stop=True),
                            [ev_r0, ev_r1] + RO[0].wbegin())
                        RO[0].wend(evb)
                        g_rowt.append(evb)
                        evbc = pr.v(lambda e, dstbc=dstbc, b=b, nh=nh: e.tensor_copy(
                            dstbc[:, C * b + 512 * nh:C * b + 512 * nh + 512],
                            psO[:, 0:512]), RO[0].r() + [evb])
                        RO[0].radd(evbc)
    ev_modT_done = ("V", pr.cnt["V"])
    for b in (range(NB) if not skip_mod else ()):
        pr.v(lambda e, b=b: e.tensor_scalar_add(
            caps[:, 16 * b:16 * b + 8], modT[:, 32 * b + 8:32 * b + 16], 1.0),
            [ev_modT_done], small=True)
        pr.v(lambda e, b=b: e.tensor_scalar_add(
            caps[:, 16 * b + 8:16 * b + 16], modT[:, 32 * b + 24:32 * b + 32], 1.0),
            [ev_modT_done], small=True)

    if "dbgD" in dbg:
        evd = ("V", pr.cnt["V"])
        ST(d_dbgD[:, 0:2048], gabc[:, :], [evd])
        ST(d_dbgD[:, 2048:4096], gbbc[:, :], [evd])
        ST(d_dbgE[:, 0:64], modT[:, :], [evd])
        ST(d_dbgE[:, 64:96], caps[:, :], [evd])
    wtmp = AFp(o_big, 3072)
    gthr = AFp(o_gthr, 128)
    gthrT = AFp(o_gthrT, 128)
    augs = AFp(o_augs, 16)
    wob = AFp(o_big2, 1024)
    castb = AB(o_cb, 3072)
    ev_st_wq = None
    ev_st_wo = None
    ev_cast = None
    for fi in range(FC if not skip_wqkv else 0):
        w = [ev_st_wq] if ev_st_wq else []
        e_ld = LD(wtmp, d_wqkv[128 * fi:128 * fi + 128, :], w, slot="LP0")
        ev_g = pr.v(lambda e: e.tensor_copy(
            gthr.rearrange("p (h d) -> p h d", d=8),
            wtmp[:, 0:C].rearrange("p (h hd) -> p h hd", hd=64)[:, :, 0:8]),
            [e_ld])
        ev_auge = None
        for h in range(H):
            evp = pr.v(lambda e, h=h: e.tensor_mul(
                gthrT[:, 8 * h:8 * h + 8], gthr[:, 8 * h:8 * h + 8],
                hsbc[:, 8 * h:8 * h + 8]), [ev_g, ev_hbc])
            ev_auge = pr.v(lambda e, h=h: e.reduce_sum(
                augs[:, h:h + 1], gthrT[:, 8 * h:8 * h + 8], axis=AX.X), [evp],
                small=True)
        ev_fold = None
        for dd in range(8):
            ev_mul = pr.v(lambda e, dd=dd: e.tensor_mul(
                gthr.rearrange("p (h d) -> p h d", d=8)[:, :, dd:dd + 1],
                dbc[:, dd:dd + 1].rearrange("p a -> p 1 a") if False else
                dbc[:, :].rearrange("p (h d) -> p h d", d=8)[:, :, dd:dd + 1],
                augs[:, :].rearrange("p (h o) -> p h o", o=1)), [ev_auge])
            ev_fold = ev_mul
        ev_add = pr.v(lambda e: e.tensor_add(
            wtmp[:, 0:C].rearrange("p (h hd) -> p h hd", hd=64)[:, :, 0:8],
            wtmp[:, 0:C].rearrange("p (h hd) -> p h hd", hd=64)[:, :, 0:8],
            gthr.rearrange("p (h d) -> p h d", d=8)), [ev_fold])
        ev_cast = pr.v(lambda e: e.tensor_copy(castb, wtmp), [ev_add])
        ev_st_wq = ST(d_wqkvs[128 * fi:128 * fi + 128, :], castb, [ev_cast],
                      slot="SC")
        w2 = [ev_st_wo] if ev_st_wo else []
        e_lo = LD(wob, d_wout[128 * fi:128 * fi + 128, :], w2, slot="LP1")
        ev_co = pr.v(lambda e: e.tensor_copy(AB(o_stg2, 1024), wob), [e_lo])
        ev_st_wo = ST(d_wouts[128 * fi:128 * fi + 128, :], AB(o_stg2, 1024),
                      [ev_co], slot="SD")

    wload = AFp(o_big, 2816)
    wintf = AFp(o_big2, 2816)
    rsw = AFp(o_rsw, 1)
    swt = AFp(o_swt, 1)
    swt2 = AFp(o_swt2, 1)
    qtmp = sq_t[:, 1024:1536]
    ev_last_st = {"g": None, "u": None, "d": None}
    ev_prev_wload = [None]
    stg_guard = {}
    stg_seq = [0]
    tstage = [AFp(0, 0), ]

    def quant_rows(elds, wl, n, sw_dst, scale_back, wintf_guards,
                   dump_first=False):
        ev_mx = pr.v(lambda e, wl=wl, n=n: e.reduce_max(
            rsw, wl[:, 0:n], axis=AX.X, apply_absolute_value=True),
            list(elds), small=True)
        ev_sw = pr.v(lambda e: e.tensor_scalar_mul(swt, rsw, 1.0 / 127.0), [ev_mx],
                     small=True)
        if sw_dst is not None:
            pr.v(lambda e, sw_dst=sw_dst: e.tensor_copy(sw_dst, swt), [ev_sw],
                 small=True)
        ev_rs = pr.v(lambda e: e.reciprocal(swt2, rsw), [ev_sw], small=True)
        ev_rs2 = pr.v(lambda e: e.tensor_scalar_mul(rsw, swt2, 127.0), [ev_rs],
                      small=True)
        evqs = []
        e0 = e1_prev = evq_prev = None
        first = [True]
        for qh in range((n + 511) // 512):
            w0, w1 = 512 * qh, min(512 * qh + 512, n)
            e0 = pr.v(lambda e, w0=w0, w1=w1, wl=wl: e.tensor_scalar_mul(
                qtmp[:, 0:w1 - w0], wl[:, w0:w1], rsw[:, 0:1]), [ev_rs2])
            if dump_first and first[0]:
                first[0] = False
                d1 = ST(d_dbgB, qtmp[:, 0:512], [e0])
                def _d2(e):
                    with nc.allow_non_contiguous_dma("debug"):
                        return e.dma_start(out=d_dbgC[:, 0:1], in_=rsw)
                def _d3(e):
                    with nc.allow_non_contiguous_dma("debug"):
                        return e.dma_start(out=d_dbgC[:, 2:3], in_=swt2)
                d2 = pr.dy(_d2, [ev_rs2])
                d3 = pr.dy(_d3, [ev_rs2])
                e0 = d1
            e1 = pr.v(lambda e, w0=w0, w1=w1: e.tensor_copy(
                xqi[:, 0:w1 - w0], qtmp[:, 0:w1 - w0]),
                [e0] + ([evq_prev] if evq_prev else []))
            gw = [wintf_guards[qh]] if wintf_guards.get(qh) else []
            if scale_back:
                evq = pr.v(lambda e, w0=w0, w1=w1: e.tensor_scalar_mul(
                    wintf[:, w0:w1], xqi[:, 0:w1 - w0], swt[:, 0:1]),
                    [e1] + gw)
            else:
                evq = pr.s(lambda e, w0=w0, w1=w1: e.copy(
                    wintf[:, w0:w1], xqi[:, 0:w1 - w0]), [e1] + gw)
            e1_prev, evq_prev = e1, evq
            evqs.append(evq)
        return evqs, e0

    tr_guards = {}
    last_wl_read = [None]

    for wsrc, wdst, swdst, tag in ((d_wg, d_wgs, swg, "g"), (d_wu, d_wus, swu, "u")):
        for icx in range(IC):
            w = [last_wl_read[0]] if last_wl_read[0] else [ev_cast]
            eldA = LD(wload[:, 0:512],
                      wsrc[128 * icx:128 * icx + 128, 0:512], w, slot="LP2")
            eldB = pr.dg(lambda e, icx=icx, wsrc=wsrc: e.dma_start(
                out=wload[:, 512:1024],
                in_=wsrc[128 * icx:128 * icx + 128, 512:1024]), w)
            evqs, last_wl_read[0] = quant_rows(
                [eldA, eldB], wload, 1024, swdst[:, icx:icx + 1],
                scale_back=False, wintf_guards=tr_guards,
                dump_first=(tag == "g" and icx == 0 and "dbgB" in dbg))
            for halfq in (0, 1):
                RR = RY if halfq == 0 else RX
                ps = psY if halfq == 0 else psX
                evt = None
                wb = RR.wbegin()
                for q in range(4):
                    fp = 4 * halfq + q
                    evt = pr.t(lambda e, fp=fp, q=q, ps=ps: e.transpose(
                        ps[:, 128 * q:128 * q + 128],
                        wintf[:, 128 * fp:128 * fp + 128], ident_f[:, :]),
                        [evqs[halfq]] + (wb if q == 0 else []))
                RR.wend(evt)
                tr_guards[halfq] = evt
                soff = o_stg + 512 * (stg_seq[0] % 2)
                eve = pr.v(lambda e, ps=ps, soff=soff: e.tensor_copy(
                    AB(soff, 512), ps[:, 0:512]),
                    RR.r() + [evt, stg_guard.get(stg_seq[0] % 2)])
                RR.radd(eve)
                stg_guard[stg_seq[0] % 2] = ST(
                    wdst[:, 128 * icx:128 * icx + 128]
                    .rearrange("(f p) i -> p f i", p=128)[:, 4 * halfq:4 * halfq + 4],
                    AB(soff, 512).rearrange("p (f i) -> p f i", i=128),
                    [eve], slot="SA" if stg_seq[0] % 2 == 0 else "SB")
                ev_last_st[tag] = stg_guard[stg_seq[0] % 2]
                stg_seq[0] += 1
    for hc in range(FC):
        w = [last_wl_read[0]]
        eldA = LD(wload[:, 0:1408], d_wd[128 * hc:128 * hc + 128, 0:1408], w,
                  slot="LP2")
        eldB = pr.dg(lambda e, hc=hc: e.dma_start(
            out=wload[:, 1408:2816],
            in_=d_wd[128 * hc:128 * hc + 128, 1408:2816]), w)
        evqs, last_wl_read[0] = quant_rows([eldA, eldB], wload, II, None,
                                           scale_back=True,
                                           wintf_guards=tr_guards)
        for halfq in range(6):
            np_ = min(4, IC - 4 * halfq)
            RR = RX if halfq % 2 == 0 else RY
            ps = psX if halfq % 2 == 0 else psY
            evt = None
            wb = RR.wbegin()
            for q in range(np_):
                ip = 4 * halfq + q
                evt = pr.t(lambda e, ip=ip, q=q, ps=ps: e.transpose(
                    ps[:, 128 * q:128 * q + 128],
                    wintf[:, 128 * ip:128 * ip + 128], ident_f[:, :]),
                    [evqs[halfq]] + (wb if q == 0 else []))
            RR.wend(evt)
            tr_guards[halfq] = evt
            soff = o_stg + 512 * (stg_seq[0] % 2)
            eve = pr.v(lambda e, ps=ps, np_=np_, soff=soff: e.tensor_copy(
                AB(soff, 128 * np_), ps[:, 0:128 * np_]),
                RR.r() + [evt, stg_guard.get(stg_seq[0] % 2)])
            RR.radd(eve)
            stg_guard[stg_seq[0] % 2] = ST(
                d_wds[:, 128 * hc:128 * hc + 128]
                .rearrange("(f p) i -> p f i", p=128)[:, 4 * halfq:4 * halfq + np_],
                AB(soff, 128 * np_).rearrange("p (f i) -> p f i", i=128),
                [eve], slot="SA" if stg_seq[0] % 2 == 0 else "SB")
            ev_last_st["d"] = stg_guard[stg_seq[0] % 2]
            stg_seq[0] += 1

    pr.rendezvous()

    def ln_group(b, g, capcol, sacol, ev_x_ready, ev_xn_guard,
                 hT_dst_fn, ev_hT_wait_extra, track_max, ht_dma, x1_dma):
        spar = 4 * (g % 2)
        ev_xn = {}
        xuse = []
        e3_hist = {}
        ered = []
        for tt in range(GT):
            xap = AFp(o_xsb + 4096 * (g % 2) + 1024 * tt, 1024)
            w = list(ev_x_ready[tt])
            sl = spar + tt
            sqs = 1024 * (tt % 2)
            e1 = pr.v(lambda e, xap=xap, sl=sl: e.reduce_sum(
                st_s1[:, sl:sl + 1], xap, axis=AX.X), w, small=True)
            e2 = pr.s(lambda e, xap=xap, sqs=sqs: e.activation(
                sq_t[:, sqs:sqs + 1024], xap, AF.Square),
                w + ([e3_hist[tt - 2]] if tt - 2 in e3_hist else []))
            e3 = pr.v(lambda e, sl=sl, sqs=sqs: e.reduce_sum(
                st_s2[:, sl:sl + 1], sq_t[:, sqs:sqs + 1024], axis=AX.X), [e2],
                small=True)
            e3_hist[tt] = e3
            ered += [e1, e3]
        s0 = spar
        e4 = pr.v(lambda e, s0=s0: e.tensor_scalar_mul(
            st_m[:, s0:s0 + 4], st_s1[:, s0:s0 + 4], 1.0 / C), ered, small=True)
        e5 = pr.v(lambda e, s0=s0: e.tensor_scalar_mul(
            st_v[:, s0:s0 + 4], st_s2[:, s0:s0 + 4], 1.0 / C), ered, small=True)
        e6 = pr.v(lambda e, s0=s0: e.tensor_mul(
            st_r[:, s0:s0 + 4], st_m[:, s0:s0 + 4], st_m[:, s0:s0 + 4]), [e4],
            small=True)
        e7 = pr.v(lambda e, s0=s0: e.tensor_sub(
            st_v[:, s0:s0 + 4], st_v[:, s0:s0 + 4], st_r[:, s0:s0 + 4]), [e5, e6],
            small=True)
        e8 = pr.s(lambda e, s0=s0: e.activation(
            st_r[:, s0:s0 + 4], st_v[:, s0:s0 + 4], AF.Sqrt, bias=epsb[:, 0:1]),
            [e7], small=True)
        e9 = pr.v(lambda e, s0=s0: e.reciprocal(
            st_n[:, s0:s0 + 4], st_r[:, s0:s0 + 4]), [e8], small=True)
        e10 = pr.v(lambda e, s0=s0: e.scalar_tensor_tensor(
            st_r[:, s0:s0 + 4], st_m[:, s0:s0 + 4], -1.0, st_n[:, s0:s0 + 4],
            ALU.mult, ALU.mult), [e9], small=True)
        for tt in range(GT):
            xap = AFp(o_xsb + 4096 * (g % 2) + 1024 * tt, 1024)
            sl = spar + tt
            ev_xn[tt] = pr.v(lambda e, xap=xap, tt=tt, sl=sl: e.tensor_scalar(
                AFp(o_xn + 1024 * tt, 1024), xap,
                st_n[:, sl:sl + 1], st_r[:, sl:sl + 1], ALU.mult, ALU.add),
                [e10] + ([ev_xn_guard] if ev_xn_guard else []))
            xuse.append(ev_xn[tt])
        ev_hT = {}
        htst = []
        last_trans = None
        for f in range(FC):
            par = f % 2
            RR = RO[par]
            evt = None
            wb = RR.wbegin()
            for tt in range(GT):
                wts = [ev_xn[tt]] + (wb if tt == 0 else [])
                evt = pr.t(lambda e, f=f, tt=tt, par=par: e.transpose(
                    psO[:, 512 * par + 128 * tt:512 * par + 128 * tt + 128],
                    AFp(o_xn + 1024 * tt + 128 * f, 128), ident_f[:, :]), wts)
            RR.wend(evt)
            last_trans = evt
            dst = hT_dst_fn(f)
            ev_hT[f] = pr.v(lambda e, f=f, par=par, dst=dst: e.tensor_scalar(
                dst, psO[:, 512 * par:512 * par + 512],
                caps[:, capcol + f:capcol + f + 1],
                modT[:, sacol + f:sacol + f + 1], ALU.mult, ALU.add),
                RR.r() + [evt] + list(ev_hT_wait_extra))
            RR.radd(ev_hT[f])
            if track_max is not None:
                em = pr.v(lambda e, dst=dst: e.reduce_max(
                    st_x[:, 0:1], dst, axis=AX.X, apply_absolute_value=True),
                    [ev_hT[f]], small=True)
                pr.v(lambda e, track_max=track_max: e.tensor_max(
                    track_max[:, 0:1], track_max[:, 0:1], st_x[:, 0:1]), [em],
                    small=True)
            if ht_dma:
                col0 = b * T + 512 * g
                htst.append(ST(d_ht[128 * f:128 * f + 128, col0:col0 + 512],
                               dst, [ev_hT[f]], slot="SH"))
        if x1_dma:
            for tt in range(GT):
                row0 = b * T + 512 * g + 128 * tt
                xap = AFp(o_xsb + 4096 * (g % 2) + 1024 * tt, 1024)
                ev = ST(d_x1[row0:row0 + 128, :], xap, [ev_xn[tt]], slot="SX")
                xuse.append(ev)
        return ev_hT, xuse, last_trans, htst

    for b in (range(NB) if "p1" in phases else ()):
        ev_vainit = pr.v(lambda e: e.memset(AB(o_va, 16640), 1.0))
        ev_xuse = {}
        ev_qkv_T_done = None
        ev_outp_T_done = None
        ev_oevac = {}
        ev_xn_guard = None
        prev_ht_guard = []
        def _x_loads(g):
            gp = g % 2
            out = []
            for tt in range(GT):
                row0 = b * T + 512 * g + 128 * tt
                w = ev_xuse.get((gp, tt), [])
                ev = LD(AFp(o_xsb + 4096 * gp + 1024 * tt, 1024),
                        d_x[row0:row0 + 128, :], w,
                        slot="LA" if gp == 0 else "LB")
                out.append([ev])
            return out

        pending_x = None
        for g in range(NG):
            gpar = g % 2
            exr = pending_x if pending_x is not None else _x_loads(g)
            pending_x = _x_loads(g + 1) if g + 1 < NG else None
            ev_hT, xuse1, ev_xn_guard, _ = ln_group(
                b, g, capcol=16 * b, sacol=32 * b,
                ev_x_ready=exr, ev_xn_guard=ev_xn_guard,
                hT_dst_fn=lambda f: AB(o_hT + 512 * f, 512),
                ev_hT_wait_extra=[ev_qkv_T_done] if ev_qkv_T_done else [],
                track_max=None, ht_dma=False, x1_dma=False)
            ev_wld = {}
            ev_blk_T = {}

            def _qkv_ld(blk, w):
                pi = blk % 2
                return LD(
                    AB(o_wqb + 4096 * pi, 4096),
                    d_wqkvs[:, 512 * blk:512 * blk + 512]
                    .rearrange("(f p) j -> p f j", p=128), w,
                    slot="LW0" if pi == 0 else "LW1")

            for blk in range(2):
                ev_wld[blk] = _qkv_ld(
                    blk, [ev_outp_T_done] if ev_outp_T_done else [])
            ev_qkv_ev = []
            for blk in range(6):
                pi = blk % 2
                wq = AB(o_wqb + 4096 * pi, 4096)
                if blk < 4:
                    for fosub in range(4):
                        fo = 4 * blk + fosub
                        bank = fo % 2
                        psreg = psS[:, 512 * bank:512 * bank + 512]
                        wb = RS[bank].wbegin()
                        evm = None
                        for f in range(FC):
                            wts = [ev_wld[blk], ev_hT[f]] + (wb if f == 0 else [])
                            evm = pr.t(lambda e, f=f, fosub=fosub, psreg=psreg, wq=wq:
                                       e.matmul(psreg,
                                                wq[:, 512 * f + 128 * fosub:
                                                   512 * f + 128 * fosub + 128],
                                                AB(o_hT + 512 * f, 512),
                                                start=(f == 0), stop=(f == FC - 1)),
                                       wts)
                        RS[bank].wend(evm)
                        if fo < 8:
                            eve = pr.s(lambda e, fo=fo, psreg=psreg: e.copy(
                                AB(o_qT + 512 * fo, 512), psreg),
                                RS[bank].r() + [evm])
                        else:
                            kc = fo - 8
                            eve = pr.s(lambda e, kc=kc, g=g, psreg=psreg: e.mul(
                                AB(o_kT + 2048 * kc + 512 * g, 512), psreg, 0.125),
                                RS[bank].r() + [evm])
                        RS[bank].radd(eve)
                        ev_qkv_ev.append(eve)
                        ev_blk_T[blk] = evm
                else:
                    vb = blk - 4
                    for ts in range(GT):
                        bank = 2 + ts % 2
                        psreg = psS[:, 512 * bank:512 * bank + 512]
                        wb = RS[bank].wbegin()
                        evm = None
                        for f in range(FC):
                            wts = [ev_wld[blk], ev_hT[f]] + (wb if f == 0 else [])
                            evm = pr.t(lambda e, f=f, ts=ts, psreg=psreg, wq=wq:
                                       e.matmul(psreg,
                                                AB(o_hT + 512 * f + 128 * ts, 128),
                                                wq[:, 512 * f:512 * f + 512],
                                                start=(f == 0), stop=(f == FC - 1)),
                                       wts)
                        RS[bank].wend(evm)
                        kt = 4 * g + ts
                        eve = pr.s(lambda e, kt=kt, vb=vb, psreg=psreg: e.copy(
                            AB(o_va + 1040 * kt, 1040)
                            .rearrange("p (h d) -> p h d", d=65)
                            [:, 8 * vb:8 * vb + 8, 0:64],
                            psreg.rearrange("p (h d) -> p h d", d=64)),
                            RS[bank].r() + [evm, ev_vainit])
                        RS[bank].radd(eve)
                        ev_qkv_ev.append(eve)
                        ev_blk_T[blk] = evm
                if blk + 2 < 6:
                    ev_wld[blk + 2] = _qkv_ld(blk + 2, [ev_blk_T[blk]])
            ev_qkv_T_done = ev_blk_T[5]
            ev_qkv_evac = ev_qkv_ev[-1]
            qkv_all = list(ev_qkv_ev)

            nkt = 4 * g + 4
            ev_pv_last = None
            pend_z = None
            prev_pv = None

            def emit_z(zA, zB, z_pv_last):
                for half, h in ((0, zA), (1, zB)):
                    kcc, off = h // 2, 64 * (h % 2)
                    oreg = psO[:, 512 * half:512 * half + 512]
                    ev_ln = pr.s(lambda e, oreg=oreg, half=half: e.activation(
                        lnzb[64:65, 512 * half:512 * half + 512], oreg[64:65, :],
                        AF.Ln), RO[half].r() + [z_pv_last], small=True)
                    RO[half].radd(ev_ln)
                    RZ = RX if half == 0 else RY
                    psZ = psX if half == 0 else psY
                    evzb = pr.t(lambda e, psZ=psZ, half=half: e.matmul(
                        psZ[0:64, :], ones_f[64:65, 0:64],
                        lnzb[64:65, 512 * half:512 * half + 512],
                        start=True, stop=True), [ev_ln] + RZ.wbegin())
                    RZ.wend(evzb)
                    evzc = pr.s(lambda e, psZ=psZ, half=half: e.activation(
                        AB(o_rzb, 1024, 0, 64)[0:64, 512 * half:512 * half + 512],
                        psZ[0:64, :], AF.Exp, scale=-1.0), RZ.r() + [evzb])
                    RZ.radd(evzc)
                    ev_oevac[h] = pr.v(lambda e, kcc=kcc, off=off, oreg=oreg,
                                       half=half: e.tensor_mul(
                        AB(o_od + 512 * kcc, 512, off, off + 64),
                        oreg[0:64, :],
                        AB(o_rzb, 1024, 0, 64)[0:64, 512 * half:512 * half + 512]),
                        [evzc])
                    RO[half].radd(ev_oevac[h])

            for hp in range(H // 2):
                hA, hB = 2 * hp, 2 * hp + 1
                kc = hp
                ev_exp = {}
                ev_pv = {}

                def emit_pv(kt):
                    nonlocal ev_pv_last
                    spar = kt % 2
                    expA, expB = ev_exp[kt]
                    for half, h in ((0, hA), (1, hB)):
                        wts = [expA if half == 0 else expB]
                        if kt == 0:
                            if h - 2 in ev_oevac:
                                wts += [ev_oevac[h - 2]]
                            wts += RO[half].wbegin()
                        evp = pr.t(lambda e, kt=kt, h=h, half=half, spar=spar:
                                   e.matmul(psO[0:65, 512 * half:512 * half + 512],
                                            AB(o_va + 1040 * kt + 65 * h, 65),
                                            AB(o_pb + 1024 * spar + 512 * half, 512),
                                            start=(kt == 0), stop=(kt == nkt - 1)),
                                   wts)
                        if half == 1:
                            ev_pv[kt] = evp
                        if kt == nkt - 1:
                            RO[half].wend(evp)
                            ev_pv_last = evp

                for kt in range(nkt):
                    spar = kt % 2
                    bA, bB = 2 * spar, 2 * spar + 1
                    wb = RS[bA].wbegin() + RS[bB].wbegin()
                    wts = list(qkv_all) + wb
                    ev_scA = pr.t(lambda e, kt=kt, kc=kc, spar=spar: e.matmul(
                        psS[:, 1024 * spar:1024 * spar + 512],
                        AB(o_kT + 2048 * kc + 128 * kt, 128, 0, 64),
                        AB(o_qT + 512 * kc, 512, 0, 64),
                        start=True, stop=True), wts)
                    ev_sc = pr.t(lambda e, kt=kt, kc=kc, spar=spar: e.matmul(
                        psS[:, 1024 * spar + 512:1024 * spar + 1024],
                        AB(o_kT + 2048 * kc + 128 * kt, 128, 64, 128),
                        AB(o_qT + 512 * kc, 512, 64, 128),
                        start=True, stop=True), [])
                    RS[bA].wend(ev_sc)
                    RS[bB].wend(ev_sc)
                    ev_mh = {0: [], 1: []}
                    r = kt - 4 * g
                    if r >= 0:
                        for half in range(2):
                            if r > 0:
                                ev_mh[half].append(pr.v(
                                    lambda e, spar=spar, half=half, r=r:
                                    e.tensor_scalar_add(
                                        psS[:, 1024 * spar + 512 * half:
                                            1024 * spar + 512 * half + 128 * r],
                                        psS[:, 1024 * spar + 512 * half:
                                            1024 * spar + 512 * half + 128 * r],
                                        -1.0e9), [ev_sc]))
                            ev_mh[half].append(pr.v(
                                lambda e, spar=spar, half=half, r=r:
                                e.tensor_add(
                                    psS[:, 1024 * spar + 512 * half + 128 * r:
                                        1024 * spar + 512 * half + 128 * r + 128],
                                    psS[:, 1024 * spar + 512 * half + 128 * r:
                                        1024 * spar + 512 * half + 128 * r + 128],
                                    cmask[:, :]), [ev_sc]))
                    for half, bk in ((0, bA), (1, bB)):
                        for evm in ev_mh[half]:
                            RS[bk].radd(evm)
                            RS[bk].wend(evm)
                    if kt >= 2:
                        emit_pv(kt - 2)
                    guard = []
                    if kt >= 2:
                        guard = [ev_pv[kt - 2]]
                    elif prev_pv is not None:
                        guard = [prev_pv[nkt - 2 + kt]]
                    ev_eA = pr.s(lambda e, spar=spar: e.activation(
                        AB(o_pb + 1024 * spar, 512),
                        psS[:, 1024 * spar:1024 * spar + 512], AF.Exp),
                        [ev_sc] + ev_mh[0] + guard)
                    ev_eB = pr.s(lambda e, spar=spar: e.activation(
                        AB(o_pb + 1024 * spar + 512, 512),
                        psS[:, 1024 * spar + 512:1024 * spar + 1024], AF.Exp),
                        ev_mh[1])
                    ev_exp[kt] = (ev_eA, ev_eB)
                    RS[bA].radd(ev_eA)
                    RS[bB].radd(ev_eB)
                    if kt == 1 and pend_z is not None:
                        emit_z(*pend_z)
                        pend_z = None
                emit_pv(nkt - 2)
                emit_pv(nkt - 1)
                prev_pv = ev_pv
                pend_z = (hA, hB, ev_pv_last)
            emit_z(*pend_z)
            pend_z = None

            ev_wo = {}
            for hb in range(2):
                ev_wo[hb] = LD(
                    AB(o_wqb + 4096 * hb, 4096),
                    d_wouts[:, 512 * hb:512 * hb + 512]
                    .rearrange("(f p) j -> p f j", p=128),
                    [ev_qkv_T_done], slot="LW0" if hb == 0 else "LW1")
            ev_x1 = {tt: [] for tt in range(GT)}
            for tt in range(GT):
                for hb in range(2):
                    bank = 2 * ((2 * tt + hb) % 2)
                    banks = (bank, bank)
                    psreg = psS[:, 512 * bank:512 * bank + 512]
                    wb = RS[bank].wbegin()
                    evm = None
                    for dc in range(FC):
                        wts = [ev_oevac[2 * dc], ev_oevac[2 * dc + 1],
                               ev_wo[hb]] + (wb if dc == 0 else [])
                        evm = pr.t(lambda e, dc=dc, tt=tt, hb=hb, psreg=psreg:
                                   e.matmul(psreg,
                                            AB(o_od + 512 * dc + 128 * tt, 128),
                                            AB(o_wqb + 4096 * hb + 512 * dc, 512),
                                            start=(dc == 0), stop=(dc == FC - 1)),
                                   wts)
                    RS[bank].wend(evm)
                    xap = AFp(o_xsb + 4096 * gpar + 1024 * tt + 512 * hb, 512)
                    ev_t = pr.v(lambda e, psreg=psreg, b=b, hb=hb: e.tensor_mul(
                        AFp(o_tev, 512), psreg,
                        gabc[:, C * b + 512 * hb:C * b + 512 * hb + 512]),
                        RS[bank].r() + [evm])
                    RS[bank].radd(ev_t)
                    ev_a = pr.v(lambda e, xap=xap: e.tensor_add(
                        xap, xap, AFp(o_tev, 512)), [ev_t])
                    ev_x1[tt].append(ev_a)
            ev_outp_T_done = evm

            ev_hT2, xuse2, ev_xn_guard, ev_ht_guard = ln_group(
                b, g, capcol=16 * b + 8, sacol=32 * b + 16,
                ev_x_ready=[ev_x1[tt] for tt in range(GT)],
                ev_xn_guard=ev_xn_guard,
                hT_dst_fn=lambda f: AB(o_ht2 + 512 * f, 512),
                ev_hT_wait_extra=list(prev_ht_guard),
                track_max=hmax1, ht_dma=True, x1_dma=True)
            prev_ht_guard = ev_ht_guard
            ev_xuse = dict(ev_xuse)
            for tt in range(GT):
                ev_xuse[(gpar, tt)] = xuse1 + xuse2
        pr.rendezvous()

    def allreduce_max(local, cci, cco, scol):
        ev1 = ST(cci.ap(), local[:, 0:1])
        evc = pr.cc(lambda e: e.collective_compute(
            "AllReduce", ALU.max, replica_groups=[list(range(8))],
            ins=[cci.ap()], outs=[cco.ap()]), [ev1])
        ev2 = pr.dg(lambda e: e.dma_start(out=gmt[:, 0:1], in_=cco.ap()), [evc])
        evt = pr.t(lambda e: e.transpose(psX[0:1, 0:128], gmt[:, 0:1], ident_f[:, :]),
                   [ev2] + RX.wbegin())
        RX.wend(evt)
        evr = pr.v(lambda e, scol=scol: e.reduce_max(
            scal[0:1, scol + 2:scol + 3], psX[0:1, 0:128], axis=AX.X),
            RX.r() + [evt], small=True)
        RX.radd(evr)
        evs = pr.v(lambda e, scol=scol: e.tensor_scalar(
            scal[0:1, scol:scol + 1], scal[0:1, scol + 2:scol + 3],
            sigx[0:1, scol // 4:scol // 4 + 1], 1.0 / 127.0, ALU.mult, ALU.mult),
            [evr], small=True)
        evrs = pr.v(lambda e, scol=scol: e.reciprocal(
            scal[0:1, scol + 1:scol + 2], scal[0:1, scol:scol + 1]), [evs],
            small=True)
        return evs, evrs

    if "p2" not in phases:
        allw = [(s, pr.cnt[s]) for s in DMA_SEMS]
        pr.op("Y", None, allw)
        pr.op("G", None, allw)
        with nc.Block() as blk:
            blk.tensor(lambda e: pr.replay("T", e, sems))
            blk.vector(lambda e: pr.replay("V", e, sems))
            blk.scalar(lambda e: pr.replay("S", e, sems))
            blk.gpsimd(lambda e: pr.replay("G", e, sems))
            blk.sync(lambda e: pr.replay("Y", e, sems))
        ctx.close()
        return nc
    evs_x, evrs_x = allreduce_max(hmax1, cc1i, cc1o, 0)
    ev_bc1 = pr.t(lambda e: e.matmul(psY[:, 0:2], ones_f[0:1, :], scal[0:1, 0:2],
                                     start=True, stop=True),
                  [evs_x, evrs_x] + RY.wbegin())
    RY.wend(ev_bc1)
    ev_sxbc = pr.v(lambda e: e.tensor_copy(sxbc[:, 0:2], psY[:, 0:2]),
                   RY.r() + [ev_bc1], small=True)
    RY.radd(ev_sxbc)
    ev_sxg = pr.v(lambda e: e.tensor_scalar(sxswg[:, :], swg[:, :], sxbc[:, 0:1],
                                            None, ALU.mult, ALU.bypass), [ev_sxbc],
                  small=True)
    ev_sxu = pr.v(lambda e: e.tensor_scalar(sxswu[:, :], swu[:, :], sxbc[:, 0:1],
                                            None, ALU.mult, ALU.bypass), [ev_sxbc],
                  small=True)

    ev_wgld = LD(AB(o_wgT, 22528), d_wgs.rearrange("(f p) i -> p f i", p=128))
    ev_wuld = LD(AB(o_wuT, 22528), d_wus.rearrange("(f p) i -> p f i", p=128))
    ev_ht_used = [None, None]
    ev_hm_st = {}
    ev_htld = {}
    ev_gu_last = {}

    def _ht_ld(gg):
        hp = gg % 2
        w = [ev_ht_used[hp]] if ev_ht_used[hp] else []
        return LD(AB(o_htb + 4096 * hp, 4096),
                  d_ht[:, 512 * gg:512 * gg + 512]
                  .rearrange("(f p) t -> p f t", p=128),
                  w, slot="LA" if hp == 0 else "LB")

    ev_htld[0] = _ht_ld(0)
    for gg in range(NB * NG):
        col0 = 512 * gg
        hpar = gg % 2
        if gg + 1 < NB * NG:
            ev_htld[gg + 1] = _ht_ld(gg + 1)
        ev_xq = {}
        for f in range(FC):
            gmm = ([ev_gu_last[gg - 1]] if f == 0 and gg - 1 in ev_gu_last
                   else [])
            e1 = pr.v(lambda e, f=f, hpar=hpar: e.tensor_scalar(
                AFp(o_tev, 512), AB(o_htb + 4096 * hpar + 512 * f, 512),
                sxbc[:, 1:2], QLO, ALU.mult, ALU.max),
                [ev_htld[gg], ev_sxbc] + gmm)
            e1b = pr.v(lambda e: e.tensor_scalar_min(
                sq_t[:, 1024:1536], AFp(o_tev, 512), QHI), [e1])
            e2 = pr.v(lambda e: e.tensor_copy(xqi[:, :], sq_t[:, 1024:1536]), [e1b])
            ev_xq[f] = pr.v(lambda e, f=f: e.tensor_copy(
                AB(o_xq + 512 * f, 512), xqi[:, :]), [e2])
        ev_ht_used[hpar] = ev_xq[FC - 1]
        for icx in range(IC):
            gbank = (icx % 2) * 2
            ubank = gbank + 1
            gps = psS[:, 512 * gbank:512 * gbank + 512]
            ups = psS[:, 512 * ubank:512 * ubank + 512]
            wbg = RS[gbank].wbegin()
            wbu = RS[ubank].wbegin()
            evg = evu = None
            for f in range(FC):
                wts = [ev_xq[f], ev_wgld] + (wbg if f == 0 else [])
                evg = pr.t(lambda e, f=f, icx=icx, gps=gps: e.matmul(
                    gps, AB(o_wgT + 2816 * f + 128 * icx, 128),
                    AB(o_xq + 512 * f, 512),
                    start=(f == 0), stop=(f == FC - 1)), wts)
            for f in range(FC):
                wts = [ev_xq[f], ev_wuld] + (wbu if f == 0 else [])
                evu = pr.t(lambda e, f=f, icx=icx, ups=ups: e.matmul(
                    ups, AB(o_wuT + 2816 * f + 128 * icx, 128),
                    AB(o_xq + 512 * f, 512),
                    start=(f == 0), stop=(f == FC - 1)), wts)
            RS[gbank].wend(evg)
            RS[ubank].wend(evu)
            ev_sil = pr.s(lambda e, icx=icx, gps=gps: e.activation(
                AB(o_sil, 512), gps, AF.Silu, scale=sxswg[:, icx:icx + 1]),
                RS[gbank].r() + [evg, ev_sxg])
            RS[gbank].radd(ev_sil)
            hmkey = gg * IC + icx
            hmslot = hmkey % 3
            ev_hm = pr.v(lambda e, icx=icx, ups=ups, hmslot=hmslot: e.scalar_tensor_tensor(
                AB(o_hm + 512 * hmslot, 512), ups, sxswu[:, icx:icx + 1],
                AB(o_sil, 512), ALU.mult, ALU.mult),
                RS[ubank].r() + [evu, ev_sil, ev_sxu] +
                ([ev_hm_st[hmkey - 3]] if hmkey - 3 in ev_hm_st else []))
            RS[ubank].radd(ev_hm)
            em = pr.v(lambda e, hmslot=hmslot: e.reduce_max(
                st_x[:, 1:2], AB(o_hm + 512 * hmslot, 512), axis=AX.X,
                apply_absolute_value=True), [ev_hm], small=True)
            pr.v(lambda e: e.tensor_max(hmax2[:, 0:1], hmax2[:, 0:1], st_x[:, 1:2]),
                 [em], small=True)
            ev_hm_st[hmkey] = ST(
                d_hm[128 * icx:128 * icx + 128, col0:col0 + 512],
                AB(o_hm + 512 * hmslot, 512), [ev_hm],
                slot=("SM0", "SM1", "SM2")[hmslot])
        ev_gu_last[gg] = evu

    pr.rendezvous()

    evs_d, evrs_d = allreduce_max(hmax2, cc2i, cc2o, 4)
    ev_bc2 = pr.t(lambda e: e.matmul(psY[:, 0:2], ones_f[0:1, :], scal[0:1, 4:6],
                                     start=True, stop=True),
                  [evs_d, evrs_d] + RY.wbegin())
    RY.wend(ev_bc2)
    ev_sdbc = pr.v(lambda e: e.tensor_copy(sdbc[:, 0:2], psY[:, 0:2]),
                   RY.r() + [ev_bc2], small=True)
    RY.radd(ev_sdbc)

    ev_wdld = LD(AB(o_wdT, 22528), d_wds.rearrange("(f p) h -> p f h", p=128))
    ev_outst_prev = {}
    ev_hmld = {}
    ev_hm_rd = {}
    ev_mm_last = {}

    def _hm_ld(gg):
        par = gg % 2
        w = [ev_hm_rd[gg - 2]] if gg - 2 in ev_hm_rd else []
        return LD(AB(o_hmg + 11264 * par, 11264),
                  d_hm[:, 512 * gg:512 * gg + 512]
                  .rearrange("(f p) t -> p f t", p=128),
                  w, slot="LP0" if par == 0 else "LP1")

    ev_hmld[0] = _hm_ld(0)
    for gg in range(NB * NG):
        b = gg // NG
        par = gg % 2
        col0 = 512 * gg
        if gg + 1 < NB * NG:
            ev_hmld[gg + 1] = _hm_ld(gg + 1)
        ev_h2 = {}
        for icx in range(IC):
            gmm = ([ev_mm_last[gg - 1]] if icx == 0 and gg - 1 in ev_mm_last
                   else [])
            e1 = pr.v(lambda e, icx=icx, par=par: e.tensor_scalar(
                AFp(o_tev, 512), AB(o_hmg + 11264 * par + 512 * icx, 512),
                sdbc[:, 1:2], QLO, ALU.mult, ALU.max),
                [ev_hmld[gg], ev_sdbc] + gmm)
            e1b = pr.v(lambda e: e.tensor_scalar_min(
                sq_t[:, 1024:1536], AFp(o_tev, 512), QHI), [e1])
            e2 = pr.v(lambda e: e.tensor_copy(xqi[:, :], sq_t[:, 1024:1536]), [e1b])
            ev_h2[icx] = pr.v(lambda e, icx=icx: e.tensor_copy(
                AB(o_h2i + 512 * icx, 512), xqi[:, :]), [e2])
        ev_hm_rd[gg] = ev_h2[IC - 1]
        for tt in range(GT):
            gt = gg * GT + tt
            row0 = col0 + 128 * tt
            x1par = gt % 2
            ev_x1ld = LD(AFp(o_x1b + 1024 * x1par, 1024), d_x1[row0:row0 + 128, :],
                         [ev_outst_prev.get(gt - 2)],
                         slot="LW0" if x1par == 0 else "LW1")
            evos = []
            for hb in range(2):
                bank = 2 * tt + hb
                bank = bank % 4
                psreg = psS[:, 512 * bank:512 * bank + 512]
                wb = RS[bank].wbegin()
                evm = None
                for icx in range(IC):
                    wts = [ev_h2[icx], ev_wdld] + (wb if icx == 0 else [])
                    evm = pr.t(lambda e, icx=icx, tt=tt, hb=hb, psreg=psreg: e.matmul(
                        psreg, AB(o_h2i + 512 * icx + 128 * tt, 128),
                        AB(o_wdT + 1024 * icx + 512 * hb, 512),
                        start=(icx == 0), stop=(icx == IC - 1)), wts)
                RS[bank].wend(evm)
                ev_t = pr.v(lambda e, psreg=psreg, b=b, hb=hb: e.scalar_tensor_tensor(
                    AFp(o_tev, 512), psreg, sdbc[:, 0:1],
                    gbbc[:, C * b + 512 * hb:C * b + 512 * hb + 512],
                    ALU.mult, ALU.mult), RS[bank].r() + [evm, ev_sdbc])
                RS[bank].radd(ev_t)
                ev_o = pr.v(lambda e, x1par=x1par, hb=hb: e.tensor_add(
                    AFp(o_outs + 1024 * x1par + 512 * hb, 512),
                    AFp(o_x1b + 1024 * x1par + 512 * hb, 512),
                    AFp(o_tev, 512)),
                    [ev_t, ev_x1ld, ev_outst_prev.get(gt - 2)])
                evos.append(ev_o)
            ev_outst_prev[gt] = ST(d_out[row0:row0 + 128, :],
                                   AFp(o_outs + 1024 * x1par, 1024), evos,
                                   slot="SO0" if x1par == 0 else "SO1")
        ev_mm_last[gg] = evm

    allw = [(s, pr.cnt[s]) for s in DMA_SEMS]
    pr.op("Y", None, allw)
    pr.op("G", None, allw)

    with nc.Block() as blk:
        blk.tensor(lambda e: pr.replay("T", e, sems))
        blk.vector(lambda e: pr.replay("V", e, sems))
        blk.scalar(lambda e: pr.replay("S", e, sems))
        blk.gpsimd(lambda e: pr.replay("G", e, sems))
        blk.sync(lambda e: pr.replay("Y", e, sems))

    ctx.close()
    return nc


_NC_CACHE = {}


def get_program(dbg=(), phases=("p0", "p1", "p2")):
    key = (tuple(sorted(dbg)), tuple(phases))
    if key not in _NC_CACHE:
        _NC_CACHE[key] = build_program(dbg, phases)
    return _NC_CACHE[key]


def kernel(L0_attn_x, L0_attn_w_qkv, L0_attn_w_out, L0_attn_head_scales,
           L0_attn_head_dirs, L0_ffn_clip_x, L0_ffn_clip_down, L0_ffn_w_gate,
           L0_ffn_w_up, L0_ffn_w_down, L0_c, L0_wmod, _dbg=(), _ret_raw=False,
           _phases=("p0", "p1", "p2"), _trace=False):
    nc = get_program(_dbg, _phases)
    x = np.ascontiguousarray(np.asarray(L0_attn_x, dtype=np.float32))
    wqkv = np.ascontiguousarray(np.asarray(L0_attn_w_qkv, dtype=np.float32))
    wout = np.ascontiguousarray(np.asarray(L0_attn_w_out, dtype=np.float32))
    hs = np.ascontiguousarray(np.asarray(L0_attn_head_scales,
                                         dtype=np.float32).reshape(H, 1))
    dirs = np.ascontiguousarray(np.asarray(L0_attn_head_dirs, dtype=np.float32))
    clipx = np.asarray(L0_ffn_clip_x, dtype=np.float32).reshape(1, 1)
    clipd = np.asarray(L0_ffn_clip_down, dtype=np.float32).reshape(1, 1)
    wg = np.ascontiguousarray(np.asarray(L0_ffn_w_gate, dtype=np.float32))
    wu = np.ascontiguousarray(np.asarray(L0_ffn_w_up, dtype=np.float32))
    wd = np.ascontiguousarray(np.asarray(L0_ffn_w_down, dtype=np.float32))
    cv = np.ascontiguousarray(np.asarray(L0_c, dtype=np.float32))
    wmod = np.ascontiguousarray(np.asarray(L0_wmod, dtype=np.float32))

    in_maps = []
    for i in range(8):
        in_maps.append({
            "x": np.ascontiguousarray(x[2 * i:2 * i + 2].reshape(NB * T, C)),
            "wqkv": wqkv, "wout": wout, "hs": hs, "dirs": dirs,
            "clipx": clipx, "clipd": clipd,
            "wg": wg, "wu": wu, "wd": wd,
            "cvec": np.ascontiguousarray(cv[2 * i:2 * i + 2]),
            "wmod": wmod,
        })
    res = run_bass_kernel_spmd(nc, in_maps, core_ids=list(range(8)),
                               trace=_trace)
    if _ret_raw:
        return res
    out = np.stack([res.results[i]["out"].reshape(NB, T, C) for i in range(8)])
    return out.reshape(B, T, C)


if __name__ == "__main__":
    nc = build_program()
    n = sum(1 for _ in nc.all_instructions())
    print("built ok, instructions:", n)


# version 09
# speedup vs baseline: 1.0119x; 1.0119x over previous
import numpy as np
import ml_dtypes
import concourse.bass as bass
import concourse.mybir as mybir
from concourse.bass_utils import run_bass_kernel_spmd

F32 = mybir.dt.float32
BF16 = mybir.dt.bfloat16
I32 = mybir.dt.int32
AF = mybir.ActivationFunctionType
ALU = mybir.AluOpType
AX = mybir.AxisListType

B, T, C, H = 16, 2048, 1024, 16
II = 2816
NB = 2
NG = T // 512
GT = 4
FC = C // 128
IC = II // 128
QLO = -128.49999
QHI = 127.49999

ENGINES = ("T", "V", "S", "G", "Y")
LOAD_SEMS = ("LA", "LB", "LW0", "LW1", "LP0", "LP1", "LP2", "LBK")
STORE_SEMS = ("SA", "SB", "SC", "SD", "SH", "SX", "SM0", "SM1", "SM2",
              "SO0", "SO1", "SBK")
DMA_SEMS = LOAD_SEMS + STORE_SEMS
BUMP_SEMS = {"LBK", "SBK"}
SEMS = ("T", "V", "S", "G", "CC") + DMA_SEMS


class Prog:

    def __init__(self):
        self.ops = {e: [] for e in ENGINES}
        self.cnt = dict.fromkeys(SEMS, 0)
        self.small_ev = {"V": set(), "S": set()}
        self.small_win = {"V": 0, "S": 0}

    def op(self, eng, fn, waits=(), inc=None, n=1):
        w = []
        for x in waits:
            if x is None:
                continue
            s, v = x
            if s in BUMP_SEMS:
                v = self.cnt[s]
            if v and v > 0:
                w.append((s, v))
        self.ops[eng].append((fn, tuple(w), inc, n))
        if inc is not None:
            self.cnt[inc] += n
            return (inc, self.cnt[inc])
        return None

    def t(self, fn, waits=()):
        return self.op("T", fn, waits, "T")

    def _vs(self, eng, fn, waits, small):
        w = list(waits)
        need = self.small_win[eng] > 0
        if not need:
            for x in w:
                if x is not None and x[0] == eng and x[1] in self.small_ev[eng]:
                    need = True
                    break
        if need:
            w.append((eng, self.cnt[eng]))
        if self.small_win[eng] > 0:
            self.small_win[eng] -= 1
        ev = self.op(eng, fn, w, eng)
        if small:
            self.small_ev[eng].add(ev[1])
            self.small_win[eng] = 2
        return ev

    def v(self, fn, waits=(), small=False):
        return self._vs("V", fn, waits, small)

    def s(self, fn, waits=(), small=False):
        return self._vs("S", fn, waits, small)

    def g(self, fn, waits=()):
        w = list(waits) + [("G", self.cnt["G"])]
        return self.op("G", fn, w, "G")

    def dy(self, fn, waits=(), slot="LBK"):
        w = list(waits)
        if slot in BUMP_SEMS:
            w.append((slot, self.cnt[slot]))
        return self.op("Y", fn, w, slot, 16)

    def dg(self, fn, waits=(), slot="SBK"):
        w = list(waits)
        if slot in BUMP_SEMS:
            w.append((slot, self.cnt[slot]))
        return self.op("G", fn, w, slot, 16)

    def cc(self, fn, waits=()):
        return self.op("G", fn, waits, "CC")

    def rendezvous(self):
        allw = [(s, self.cnt[s]) for s in ("T", "V", "S", "G") + DMA_SEMS]
        for e in ENGINES:
            self.op(e, None, allw)
        return allw

    def replay(self, ename, eng, sems):
        hw = dict.fromkeys(SEMS, 0)
        own = {"T": "T", "V": "V", "S": "S", "G": "G", "Y": None}[ename]
        skip_own = ename in ("T", "Y")
        for fn, waits, inc, n in self.ops[ename]:
            for s, v in waits:
                if s == own and skip_own:
                    continue
                if v > hw[s]:
                    eng.wait_ge(sems[s], v)
                    hw[s] = v
            if fn is None:
                continue
            inst = fn(eng)
            if inc is not None:
                inst.then_inc(sems[inc], n)


class Reg:

    def __init__(self):
        self.readers = []
        self.writer = None

    def wbegin(self):
        w = list(self.readers)
        self.readers = []
        return w

    def wend(self, ev):
        self.writer = ev

    def r(self, ev_reader_returns=None):
        return [self.writer] if self.writer else []

    def radd(self, ev):
        self.readers.append(ev)


def build_program(dbg=(), phases=("p0", "p1", "p2"), racecheck=False):
    nc = bass.Bass("TRN2", target_bir_lowering=False,
                   detect_race_conditions=racecheck)

    def dram(name, shape, dt):
        if name in dbg:
            return nc.dram_tensor(name, shape, dt, kind="ExternalOutput").ap()
        return nc.dram_tensor(name, shape, dt).ap()

    d_x = nc.dram_tensor("x", [NB * T, C], F32, kind="ExternalInput").ap()
    d_wqkv = nc.dram_tensor("wqkv", [C, 3 * C], F32, kind="ExternalInput").ap()
    d_wout = nc.dram_tensor("wout", [C, C], F32, kind="ExternalInput").ap()
    d_hs = nc.dram_tensor("hs", [H, 1], F32, kind="ExternalInput").ap()
    d_dirs = nc.dram_tensor("dirs", [H, 8], F32, kind="ExternalInput").ap()
    d_clipx = nc.dram_tensor("clipx", [1, 1], F32, kind="ExternalInput").ap()
    d_clipd = nc.dram_tensor("clipd", [1, 1], F32, kind="ExternalInput").ap()
    d_wg = nc.dram_tensor("wg", [II, C], F32, kind="ExternalInput").ap()
    d_wu = nc.dram_tensor("wu", [II, C], F32, kind="ExternalInput").ap()
    d_wd = nc.dram_tensor("wd", [C, II], F32, kind="ExternalInput").ap()
    d_c = nc.dram_tensor("cvec", [NB, C], F32, kind="ExternalInput").ap()
    d_wmod = nc.dram_tensor("wmod", [C, 6 * C], F32, kind="ExternalInput").ap()
    d_out = nc.dram_tensor("out", [NB * T, C], F32, kind="ExternalOutput").ap()

    d_wqkvs = dram("wqkvs", [C, 3 * C], BF16)
    d_wouts = dram("wouts", [C, C], BF16)
    d_wgs = dram("wgs", [C, II], BF16)
    d_wus = dram("wus", [C, II], BF16)
    d_wds = dram("wds", [II, C], BF16)
    d_x1 = dram("x1s", [NB * T, C], F32)
    d_ht = dram("hts", [C, NB * T], BF16)
    d_hm = dram("hms", [II, NB * T], BF16)
    d_dbgA = dram("dbgA", [128, 512], F32)
    d_dbgB = dram("dbgB", [128, 512], F32)
    d_dbgC = dram("dbgC", [128, 512], F32)
    d_dbgD = dram("dbgD", [128, 4096], F32)
    d_dbgE = dram("dbgE", [128, 96], F32)
    cc1i = nc.dram_tensor("cc1i", [128, 1], F32)
    cc1o = nc.dram_tensor("cc1o", [128, 1], F32)
    cc2i = nc.dram_tensor("cc2i", [128, 1], F32)
    cc2o = nc.dram_tensor("cc2o", [128, 1], F32)

    from contextlib import ExitStack
    ctx = ExitStack()
    en = ctx.enter_context

    ident_f = en(nc.sbuf_tensor("ident_f", [128, 128], F32))
    ones_b = en(nc.sbuf_tensor("ones_b", [128, 128], BF16))
    ones_f = en(nc.sbuf_tensor("ones_f", [128, 128], F32))
    cmask = en(nc.sbuf_tensor("cmask", [128, 128], F32))
    dirsb = en(nc.sbuf_tensor("dirsb", [H, 8], F32))
    hsb = en(nc.sbuf_tensor("hsb", [H, 1], F32))
    hsdir8 = en(nc.sbuf_tensor("hsdir8", [H, 8], F32))
    dirrow = en(nc.sbuf_tensor("dirrow", [1, 128], F32))
    hsrow = en(nc.sbuf_tensor("hsrow", [1, 128], F32))
    dbc = en(nc.sbuf_tensor("dbc", [128, 128], F32))
    hsbc = en(nc.sbuf_tensor("hsbc", [128, 128], F32))
    sigx = en(nc.sbuf_tensor("sigx", [1, 2], F32))
    modT = en(nc.sbuf_tensor("modT", [128, 64], F32))
    caps = en(nc.sbuf_tensor("caps", [128, 32], F32))
    gabc = en(nc.sbuf_tensor("gabc", [128, NB * C], F32))
    gbbc = en(nc.sbuf_tensor("gbbc", [128, NB * C], F32))
    mrow = en(nc.sbuf_tensor("mrow", [1, 512], F32))
    swg = en(nc.sbuf_tensor("swg", [128, IC], F32))
    swu = en(nc.sbuf_tensor("swu", [128, IC], F32))
    sxswg = en(nc.sbuf_tensor("sxswg", [128, IC], F32))
    sxswu = en(nc.sbuf_tensor("sxswu", [128, IC], F32))
    swd1 = en(nc.sbuf_tensor("swd1", [128, 1], F32))
    hmax1 = en(nc.sbuf_tensor("hmax1", [128, 1], F32))
    hmax2 = en(nc.sbuf_tensor("hmax2", [128, 1], F32))
    gmt = en(nc.sbuf_tensor("gmt", [128, 1], F32))
    scal = en(nc.sbuf_tensor("scal", [1, 8], F32))
    sxbc = en(nc.sbuf_tensor("sxbc", [128, 2], F32))
    sdbc = en(nc.sbuf_tensor("sdbc", [128, 2], F32))
    st_s1 = en(nc.sbuf_tensor("st_s1", [128, 8], F32))
    st_s2 = en(nc.sbuf_tensor("st_s2", [128, 8], F32))
    st_m = en(nc.sbuf_tensor("st_m", [128, 8], F32))
    st_v = en(nc.sbuf_tensor("st_v", [128, 8], F32))
    st_r = en(nc.sbuf_tensor("st_r", [128, 8], F32))
    st_n = en(nc.sbuf_tensor("st_n", [128, 8], F32))
    st_x = en(nc.sbuf_tensor("st_x", [128, 2], F32))
    epsb = en(nc.sbuf_tensor("epsb", [128, 1], F32))
    lnzb = en(nc.sbuf_tensor("lnzb", [128, 1024], F32))
    sq_t = en(nc.sbuf_tensor("sq_t", [128, 2 * C], F32))

    ar_b = en(nc.sbuf_tensor("ar_b", [128, 60700], BF16))
    ar_f = en(nc.sbuf_tensor("ar_f", [128, 12850], F32))
    xqi = en(nc.sbuf_tensor("xqi", [128, 512], I32))

    psS = en(nc.psum_tensor("psS", [128, 2048], F32))
    psO = en(nc.psum_tensor("psO", [128, 1024], F32))
    psX = en(nc.psum_tensor("psX", [128, 512], F32))
    psY = en(nc.psum_tensor("psY", [128, 512], F32))

    sems = {s: en(nc.semaphore("sem_" + s)) for s in SEMS}
    pr = Prog()

    RS = [Reg() for _ in range(4)]
    RO = [Reg() for _ in range(2)]
    RX = Reg()
    RY = Reg()

    def LD(dst, src, waits=(), slot="LBK"):
        return pr.dy(lambda e, d=dst, s=src: e.dma_start(out=d, in_=s), waits, slot)

    def ST(dst, src, waits=(), slot="SBK"):
        return pr.dy(lambda e, d=dst, s=src: e.dma_start(out=d, in_=s), waits, slot)

    def carve(n, *sizes):
        out, off = [], 0
        for s in sizes:
            out.append(off)
            off += s
        assert off <= n, (off, n)
        return out

    (o_wqb, o_kT, o_va, o_qT, o_hT, o_pb, o_od, o_rzb, o_ht2) = carve(
        60700, 8192, 16384, 16640, 4096, 4096, 2048, 4096, 1024, 4096)
    (o_wgT, o_wuT, o_htb, o_xq, o_hm, o_sil) = carve(
        60700, 22528, 22528, 8192, 4096, 1536, 512)
    (o_wdT, o_hmg, o_h2i) = carve(60700, 22528, 2 * 11264, 11264)
    (o_xsb, o_xn, o_tev) = carve(12850, 8192, 4096, 512)
    (o_x1b, o_outs) = carve(12850, 2048, 2048)
    (o_wmA, o_wmB, o_crow, o_scT, o_hsdT, o_big, o_pad_, o_big2,
     o_gthr, o_gthrT, o_augs, o_rsw, o_swt, o_swt2, o_rowt) = carve(
        12850, 512, 512, 1024, 16, 16, 3072, 160, 3072, 128, 128, 16, 1, 1, 1, 2048)
    rowt = ar_f[0:1, o_rowt:o_rowt + NB * C]
    (o_cb, o_stg, o_stg2) = carve(60700, 3072, 1024, 1024)

    def AB(off, ln, p0=0, p1=128):
        return ar_b[p0:p1, off:off + ln]

    def AFp(off, ln, p0=0, p1=128):
        return ar_f[p0:p1, off:off + ln]

    pr.g(lambda e: e.memset(ident_f[:, :], 0.0))
    pr.g(lambda e: e.affine_select(out=ident_f[:, :], in_=ident_f[:, :],
                                   compare_op=ALU.not_equal, fill=1.0,
                                   base=0, pattern=[[-1, 128]], channel_multiplier=1))
    pr.g(lambda e: e.memset(ones_b[:, :], 1.0))
    pr.g(lambda e: e.memset(ones_f[:, :], 1.0))
    pr.g(lambda e: e.memset(cmask[:, :], 0.0))
    pr.g(lambda e: e.affine_select(out=cmask[:, :], in_=cmask[:, :],
                                   compare_op=ALU.is_ge, fill=-1.0e9,
                                   base=0, pattern=[[1, 128]], channel_multiplier=-1))
    pr.g(lambda e: e.memset(epsb[:, :], 1e-5))
    pr.g(lambda e: e.memset(hmax1[:, :], 0.0))
    pr.g(lambda e: e.memset(hmax2[:, :], 0.0))
    ev_gc = ("G", pr.cnt["G"])

    e_hs = LD(hsb[:, :], d_hs)
    e_dirs = LD(dirsb[:, :], d_dirs)
    LD(sigx[0:1, 0:1], d_clipx)
    e_clip2 = LD(sigx[0:1, 1:2], d_clipd)
    pr.s(lambda e: e.activation(sigx[0:1, 0:2], sigx[0:1, 0:2], AF.Sigmoid),
         [e_clip2], small=True)
    ev_hsd = pr.v(lambda e: e.tensor_scalar(hsdir8[:, :], dirsb[:, :], hsb[:, 0:1],
                                            8.0, ALU.mult, ALU.mult),
                  [e_dirs, e_hs], small=True)
    ev_dr = pr.dg(lambda e: e.dma_start(
        out=dirrow[0:1, :].rearrange("a (h d) -> a h d", d=8),
        in_=dirsb[:, :]), [e_dirs])
    ev_hr = pr.dg(lambda e: e.dma_start(
        out=hsrow[0:1, :].rearrange("a (h d) -> a h d", d=8),
        in_=hsdir8[:, :]), [ev_hsd])
    ev_dbcm = pr.t(lambda e: e.matmul(psX[:, 0:128], ones_f[0:1, :], dirrow[0:1, :],
                                      start=True, stop=True),
                   [ev_dr, ev_gc] + RX.wbegin())
    RX.wend(ev_dbcm)
    ev_dbc = pr.v(lambda e: e.tensor_copy(dbc[:, :], psX[:, 0:128]), RX.r() + [ev_dbcm])
    RX.radd(ev_dbc)
    ev_hbcm = pr.t(lambda e: e.matmul(psY[:, 0:128], ones_f[0:1, :], hsrow[0:1, :],
                                      start=True, stop=True),
                   [ev_hr, ev_gc] + RY.wbegin())
    RY.wend(ev_hbcm)
    ev_hbc = pr.v(lambda e: e.tensor_copy(hsbc[:, :], psY[:, 0:128]), RY.r() + [ev_hbcm])
    RY.radd(ev_hbc)

    if "dbgA" in dbg:
        pr.dy(lambda e: e.dma_start(out=d_dbgA[:, 288:416], in_=ident_f[:, :]),
              [ev_gc, ("G", pr.cnt["G"])])
    crow = AFp(o_crow, C, 0, NB)
    scT = AFp(o_scT, 16)
    skip_mod = "nomod" in dbg
    skip_wqkv = "nowqkv" in dbg
    e_c = LD(crow, d_c)
    ev_silu = pr.s(lambda e: e.activation(crow, crow, AF.Silu), [e_c])
    ev_sct = None
    for ccx in range(FC if not skip_mod else 0):
        evt = pr.t(lambda e, ccx=ccx: e.transpose(
            psO[0:128, 0:2], crow[0:NB, 128 * ccx:128 * ccx + 128], ident_f[0:NB, 0:NB]),
            [ev_silu, ev_gc] + RO[0].wbegin())
        RO[0].wend(evt)
        ev_sct = pr.v(lambda e, ccx=ccx: e.tensor_copy(scT[:, 2 * ccx:2 * ccx + 2],
                                                       psO[:, 0:2]),
                      RO[0].r() + [evt])
        RO[0].radd(ev_sct)

    wmb = [AFp(o_wmA, 512), AFp(o_wmB, 512)]
    ev_wm_use = [None, None]
    ldseq = 0
    g_mrow = []
    g_rowt = []
    for nch in range(12 if not skip_mod else 0):
        evmm = [None, None]
        wx = RX.wbegin() + RY.wbegin()
        for ccx in range(FC):
            pi = ldseq % 2
            ldseq += 1
            w = [ev_sct]
            if ev_wm_use[pi] is not None:
                w.append(ev_wm_use[pi])
            evld = LD(wmb[pi], d_wmod[128 * ccx:128 * ccx + 128,
                                      512 * nch:512 * nch + 512], w,
                      slot="LA" if pi == 0 else "LB")
            for b in range(NB):
                ps = psX if b == 0 else psY
                evmm[b] = pr.t(lambda e, ccx=ccx, b=b, pi=pi, ps=ps: e.matmul(
                    ps[0:1, 0:512], scT[:, 2 * ccx + b:2 * ccx + b + 1], wmb[pi],
                    start=(ccx == 0), stop=(ccx == FC - 1)),
                    [evld] + (wx if ccx == 0 else []))
            ev_wm_use[pi] = evmm[1]
        RX.wend(evmm[0])
        RY.wend(evmm[1])
        gidx, half = nch // 2, nch % 2
        for b in range(NB):
            ps = psX if b == 0 else psY
            RR = RX if b == 0 else RY
            ev_row = pr.v(lambda e, ps=ps: e.tensor_copy(mrow[0:1, :], ps[0:1, 0:512]),
                          RR.r() + [evmm[b]] + g_mrow, small=True)
            g_mrow = []
            RR.radd(ev_row)
            if gidx in (0, 1, 3, 4):
                gslot = {0: 0, 1: 1, 3: 2, 4: 3}[gidx]
                for q in range(4):
                    col = 32 * b + gslot * 8 + half * 4 + q
                    evt = pr.t(lambda e, q=q: e.transpose(
                        psO[0:128, 512:513], mrow[0:1, 128 * q:128 * q + 128],
                        ident_f[0:1, 0:1]), [ev_row] + RO[1].wbegin())
                    RO[1].wend(evt)
                    g_mrow.append(evt)
                    evc = pr.v(lambda e, col=col: e.tensor_copy(
                        modT[:, col:col + 1], psO[:, 512:513]), RO[1].r() + [evt],
                        small=True)
                    RO[1].radd(evc)
            else:
                evcp = pr.v(lambda e, half=half, b=b: e.tensor_copy(
                    rowt[0:1, C * b + 512 * half:C * b + 512 * half + 512],
                    mrow[0:1, :]),
                    [ev_row] + g_rowt, small=True)
                g_rowt = []
                g_mrow.append(evcp)
                if half == 1:
                    rb0, rb1 = C * b, C * b + 512
                    ev_a = pr.s(lambda e, rb0=rb0: e.activation(
                        sq_t[0:1, 0:C], rowt[0:1, rb0:rb0 + C], AF.Abs), [evcp], small=True)
                    ev_m = pr.v(lambda e: e.tensor_scalar_max(
                        sq_t[0:1, 0:C], sq_t[0:1, 0:C], 0.25), [ev_a], small=True)
                    ev_sg = pr.s(lambda e, rb0=rb0: e.sign(
                        mrow[0:1, :], rowt[0:1, rb0:rb0 + 512]), [ev_m], small=True)
                    g_mrow.append(ev_sg)
                    ev_sg2 = pr.s(lambda e, rb0=rb0, rb1=rb1: e.sign(
                        rowt[0:1, rb0:rb0 + 512], rowt[0:1, rb1:rb1 + 512]), [ev_sg], small=True)
                    ev_r1 = pr.v(lambda e, rb0=rb0, rb1=rb1: e.tensor_mul(
                        rowt[0:1, rb1:rb1 + 512], sq_t[0:1, 512:C],
                        rowt[0:1, rb0:rb0 + 512]), [ev_sg2, ev_m], small=True)
                    ev_r0 = pr.v(lambda e, rb0=rb0: e.tensor_mul(
                        rowt[0:1, rb0:rb0 + 512], sq_t[0:1, 0:512], mrow[0:1, :]),
                        [ev_sg, ev_r1, ev_m], small=True)
                    g_mrow.append(ev_r0)
                    dstbc = gabc if gidx == 2 else gbbc
                    for nh in range(2):
                        evb = pr.t(lambda e, nh=nh, rb0=rb0: e.matmul(
                            psO[:, 0:512], ones_f[0:1, :],
                            rowt[0:1, rb0 + 512 * nh:rb0 + 512 * nh + 512],
                            start=True, stop=True),
                            [ev_r0, ev_r1] + RO[0].wbegin())
                        RO[0].wend(evb)
                        g_rowt.append(evb)
                        evbc = pr.v(lambda e, dstbc=dstbc, b=b, nh=nh: e.tensor_copy(
                            dstbc[:, C * b + 512 * nh:C * b + 512 * nh + 512],
                            psO[:, 0:512]), RO[0].r() + [evb])
                        RO[0].radd(evbc)
    ev_modT_done = ("V", pr.cnt["V"])
    for b in (range(NB) if not skip_mod else ()):
        pr.v(lambda e, b=b: e.tensor_scalar_add(
            caps[:, 16 * b:16 * b + 8], modT[:, 32 * b + 8:32 * b + 16], 1.0),
            [ev_modT_done], small=True)
        pr.v(lambda e, b=b: e.tensor_scalar_add(
            caps[:, 16 * b + 8:16 * b + 16], modT[:, 32 * b + 24:32 * b + 32], 1.0),
            [ev_modT_done], small=True)

    if "dbgD" in dbg:
        evd = ("V", pr.cnt["V"])
        ST(d_dbgD[:, 0:2048], gabc[:, :], [evd])
        ST(d_dbgD[:, 2048:4096], gbbc[:, :], [evd])
        ST(d_dbgE[:, 0:64], modT[:, :], [evd])
        ST(d_dbgE[:, 64:96], caps[:, :], [evd])
    wtmp = AFp(o_big, 3072)
    gthr = AFp(o_gthr, 128)
    gthrT = AFp(o_gthrT, 128)
    augs = AFp(o_augs, 16)
    wob = AFp(o_big2, 1024)
    castb = AB(o_cb, 3072)
    ev_st_wq = None
    ev_st_wo = None
    ev_cast = None
    for fi in range(FC if not skip_wqkv else 0):
        w = [ev_st_wq] if ev_st_wq else []
        e_ld = LD(wtmp, d_wqkv[128 * fi:128 * fi + 128, :], w, slot="LP0")
        ev_g = pr.v(lambda e: e.tensor_copy(
            gthr.rearrange("p (h d) -> p h d", d=8),
            wtmp[:, 0:C].rearrange("p (h hd) -> p h hd", hd=64)[:, :, 0:8]),
            [e_ld])
        ev_auge = None
        for h in range(H):
            evp = pr.v(lambda e, h=h: e.tensor_mul(
                gthrT[:, 8 * h:8 * h + 8], gthr[:, 8 * h:8 * h + 8],
                hsbc[:, 8 * h:8 * h + 8]), [ev_g, ev_hbc])
            ev_auge = pr.v(lambda e, h=h: e.reduce_sum(
                augs[:, h:h + 1], gthrT[:, 8 * h:8 * h + 8], axis=AX.X), [evp],
                small=True)
        ev_fold = None
        for dd in range(8):
            ev_mul = pr.v(lambda e, dd=dd: e.tensor_mul(
                gthr.rearrange("p (h d) -> p h d", d=8)[:, :, dd:dd + 1],
                dbc[:, dd:dd + 1].rearrange("p a -> p 1 a") if False else
                dbc[:, :].rearrange("p (h d) -> p h d", d=8)[:, :, dd:dd + 1],
                augs[:, :].rearrange("p (h o) -> p h o", o=1)), [ev_auge])
            ev_fold = ev_mul
        ev_add = pr.v(lambda e: e.tensor_add(
            wtmp[:, 0:C].rearrange("p (h hd) -> p h hd", hd=64)[:, :, 0:8],
            wtmp[:, 0:C].rearrange("p (h hd) -> p h hd", hd=64)[:, :, 0:8],
            gthr.rearrange("p (h d) -> p h d", d=8)), [ev_fold])
        ev_cast = pr.v(lambda e: e.tensor_copy(castb, wtmp), [ev_add])
        ev_st_wq = ST(d_wqkvs[128 * fi:128 * fi + 128, :], castb, [ev_cast],
                      slot="SC")
        w2 = [ev_st_wo] if ev_st_wo else []
        e_lo = LD(wob, d_wout[128 * fi:128 * fi + 128, :], w2, slot="LP1")
        ev_co = pr.v(lambda e: e.tensor_copy(AB(o_stg2, 1024), wob), [e_lo])
        ev_st_wo = ST(d_wouts[128 * fi:128 * fi + 128, :], AB(o_stg2, 1024),
                      [ev_co], slot="SD")

    wload = AFp(o_big, 2816)
    wintf = AFp(o_big2, 2816)
    rsw = AFp(o_rsw, 1)
    swt = AFp(o_swt, 1)
    swt2 = AFp(o_swt2, 1)
    qtmp = sq_t[:, 1024:1536]
    ev_last_st = {"g": None, "u": None, "d": None}
    ev_prev_wload = [None]
    stg_guard = {}
    stg_seq = [0]
    tstage = [AFp(0, 0), ]

    def quant_rows(elds, wl, n, sw_dst, scale_back, wintf_guards,
                   dump_first=False):
        ev_mx = pr.v(lambda e, wl=wl, n=n: e.reduce_max(
            rsw, wl[:, 0:n], axis=AX.X, apply_absolute_value=True),
            list(elds), small=True)
        ev_sw = pr.v(lambda e: e.tensor_scalar_mul(swt, rsw, 1.0 / 127.0), [ev_mx],
                     small=True)
        if sw_dst is not None:
            pr.v(lambda e, sw_dst=sw_dst: e.tensor_copy(sw_dst, swt), [ev_sw],
                 small=True)
        ev_rs = pr.v(lambda e: e.reciprocal(swt2, rsw), [ev_sw], small=True)
        ev_rs2 = pr.v(lambda e: e.tensor_scalar_mul(rsw, swt2, 127.0), [ev_rs],
                      small=True)
        evqs = []
        e0 = e1_prev = evq_prev = None
        first = [True]
        for qh in range((n + 511) // 512):
            w0, w1 = 512 * qh, min(512 * qh + 512, n)
            e0 = pr.v(lambda e, w0=w0, w1=w1, wl=wl: e.tensor_scalar_mul(
                qtmp[:, 0:w1 - w0], wl[:, w0:w1], rsw[:, 0:1]), [ev_rs2])
            if dump_first and first[0]:
                first[0] = False
                d1 = ST(d_dbgB, qtmp[:, 0:512], [e0])
                def _d2(e):
                    with nc.allow_non_contiguous_dma("debug"):
                        return e.dma_start(out=d_dbgC[:, 0:1], in_=rsw)
                def _d3(e):
                    with nc.allow_non_contiguous_dma("debug"):
                        return e.dma_start(out=d_dbgC[:, 2:3], in_=swt2)
                d2 = pr.dy(_d2, [ev_rs2])
                d3 = pr.dy(_d3, [ev_rs2])
                e0 = d1
            e1 = pr.v(lambda e, w0=w0, w1=w1: e.tensor_copy(
                xqi[:, 0:w1 - w0], qtmp[:, 0:w1 - w0]),
                [e0] + ([evq_prev] if evq_prev else []))
            gw = [wintf_guards[qh]] if wintf_guards.get(qh) else []
            if scale_back:
                evq = pr.v(lambda e, w0=w0, w1=w1: e.tensor_scalar_mul(
                    wintf[:, w0:w1], xqi[:, 0:w1 - w0], swt[:, 0:1]),
                    [e1] + gw)
            else:
                evq = pr.s(lambda e, w0=w0, w1=w1: e.copy(
                    wintf[:, w0:w1], xqi[:, 0:w1 - w0]), [e1] + gw)
            e1_prev, evq_prev = e1, evq
            evqs.append(evq)
        return evqs, e0

    tr_guards = {}
    last_wl_read = [None]

    for wsrc, wdst, swdst, tag in ((d_wg, d_wgs, swg, "g"), (d_wu, d_wus, swu, "u")):
        for icx in range(IC):
            w = [last_wl_read[0]] if last_wl_read[0] else [ev_cast]
            eldA = LD(wload[:, 0:512],
                      wsrc[128 * icx:128 * icx + 128, 0:512], w, slot="LP2")
            eldB = pr.dg(lambda e, icx=icx, wsrc=wsrc: e.dma_start(
                out=wload[:, 512:1024],
                in_=wsrc[128 * icx:128 * icx + 128, 512:1024]), w)
            evqs, last_wl_read[0] = quant_rows(
                [eldA, eldB], wload, 1024, swdst[:, icx:icx + 1],
                scale_back=False, wintf_guards=tr_guards,
                dump_first=(tag == "g" and icx == 0 and "dbgB" in dbg))
            for halfq in (0, 1):
                RR = RY if halfq == 0 else RX
                ps = psY if halfq == 0 else psX
                evt = None
                wb = RR.wbegin()
                for q in range(4):
                    fp = 4 * halfq + q
                    evt = pr.t(lambda e, fp=fp, q=q, ps=ps: e.transpose(
                        ps[:, 128 * q:128 * q + 128],
                        wintf[:, 128 * fp:128 * fp + 128], ident_f[:, :]),
                        [evqs[halfq]] + (wb if q == 0 else []))
                RR.wend(evt)
                tr_guards[halfq] = evt
                soff = o_stg + 512 * (stg_seq[0] % 2)
                eve = pr.v(lambda e, ps=ps, soff=soff: e.tensor_copy(
                    AB(soff, 512), ps[:, 0:512]),
                    RR.r() + [evt, stg_guard.get(stg_seq[0] % 2)])
                RR.radd(eve)
                stg_guard[stg_seq[0] % 2] = ST(
                    wdst[:, 128 * icx:128 * icx + 128]
                    .rearrange("(f p) i -> p f i", p=128)[:, 4 * halfq:4 * halfq + 4],
                    AB(soff, 512).rearrange("p (f i) -> p f i", i=128),
                    [eve], slot="SA" if stg_seq[0] % 2 == 0 else "SB")
                ev_last_st[tag] = stg_guard[stg_seq[0] % 2]
                stg_seq[0] += 1
    for hc in range(FC):
        w = [last_wl_read[0]]
        eldA = LD(wload[:, 0:1408], d_wd[128 * hc:128 * hc + 128, 0:1408], w,
                  slot="LP2")
        eldB = pr.dg(lambda e, hc=hc: e.dma_start(
            out=wload[:, 1408:2816],
            in_=d_wd[128 * hc:128 * hc + 128, 1408:2816]), w)
        evqs, last_wl_read[0] = quant_rows([eldA, eldB], wload, II, None,
                                           scale_back=True,
                                           wintf_guards=tr_guards)
        for halfq in range(6):
            np_ = min(4, IC - 4 * halfq)
            RR = RX if halfq % 2 == 0 else RY
            ps = psX if halfq % 2 == 0 else psY
            evt = None
            wb = RR.wbegin()
            for q in range(np_):
                ip = 4 * halfq + q
                evt = pr.t(lambda e, ip=ip, q=q, ps=ps: e.transpose(
                    ps[:, 128 * q:128 * q + 128],
                    wintf[:, 128 * ip:128 * ip + 128], ident_f[:, :]),
                    [evqs[halfq]] + (wb if q == 0 else []))
            RR.wend(evt)
            tr_guards[halfq] = evt
            soff = o_stg + 512 * (stg_seq[0] % 2)
            eve = pr.v(lambda e, ps=ps, np_=np_, soff=soff: e.tensor_copy(
                AB(soff, 128 * np_), ps[:, 0:128 * np_]),
                RR.r() + [evt, stg_guard.get(stg_seq[0] % 2)])
            RR.radd(eve)
            stg_guard[stg_seq[0] % 2] = ST(
                d_wds[:, 128 * hc:128 * hc + 128]
                .rearrange("(f p) i -> p f i", p=128)[:, 4 * halfq:4 * halfq + np_],
                AB(soff, 128 * np_).rearrange("p (f i) -> p f i", i=128),
                [eve], slot="SA" if stg_seq[0] % 2 == 0 else "SB")
            ev_last_st["d"] = stg_guard[stg_seq[0] % 2]
            stg_seq[0] += 1

    pr.rendezvous()

    def ln_group(b, g, capcol, sacol, ev_x_ready, ev_xn_guard,
                 hT_dst_fn, ev_hT_wait_extra, track_max, ht_dma, x1_dma):
        spar = 4 * (g % 2)
        ev_xn = {}
        xuse = []
        e3_hist = {}
        ered = []
        for tt in range(GT):
            xap = AFp(o_xsb + 4096 * (g % 2) + 1024 * tt, 1024)
            w = list(ev_x_ready[tt])
            sl = spar + tt
            sqs = 1024 * (tt % 2)
            e1 = pr.v(lambda e, xap=xap, sl=sl: e.reduce_sum(
                st_s1[:, sl:sl + 1], xap, axis=AX.X), w, small=True)
            e2 = pr.s(lambda e, xap=xap, sqs=sqs: e.activation(
                sq_t[:, sqs:sqs + 1024], xap, AF.Square),
                w + ([e3_hist[tt - 2]] if tt - 2 in e3_hist else []))
            e3 = pr.v(lambda e, sl=sl, sqs=sqs: e.reduce_sum(
                st_s2[:, sl:sl + 1], sq_t[:, sqs:sqs + 1024], axis=AX.X), [e2],
                small=True)
            e3_hist[tt] = e3
            ered += [e1, e3]
        s0 = spar
        e4 = pr.v(lambda e, s0=s0: e.tensor_scalar_mul(
            st_m[:, s0:s0 + 4], st_s1[:, s0:s0 + 4], 1.0 / C), ered, small=True)
        e5 = pr.v(lambda e, s0=s0: e.tensor_scalar_mul(
            st_v[:, s0:s0 + 4], st_s2[:, s0:s0 + 4], 1.0 / C), ered, small=True)
        e6 = pr.v(lambda e, s0=s0: e.tensor_mul(
            st_r[:, s0:s0 + 4], st_m[:, s0:s0 + 4], st_m[:, s0:s0 + 4]), [e4],
            small=True)
        e7 = pr.v(lambda e, s0=s0: e.tensor_sub(
            st_v[:, s0:s0 + 4], st_v[:, s0:s0 + 4], st_r[:, s0:s0 + 4]), [e5, e6],
            small=True)
        e8 = pr.s(lambda e, s0=s0: e.activation(
            st_r[:, s0:s0 + 4], st_v[:, s0:s0 + 4], AF.Sqrt, bias=epsb[:, 0:1]),
            [e7], small=True)
        e9 = pr.v(lambda e, s0=s0: e.reciprocal(
            st_n[:, s0:s0 + 4], st_r[:, s0:s0 + 4]), [e8], small=True)
        e10 = pr.v(lambda e, s0=s0: e.scalar_tensor_tensor(
            st_r[:, s0:s0 + 4], st_m[:, s0:s0 + 4], -1.0, st_n[:, s0:s0 + 4],
            ALU.mult, ALU.mult), [e9], small=True)
        for tt in range(GT):
            xap = AFp(o_xsb + 4096 * (g % 2) + 1024 * tt, 1024)
            sl = spar + tt
            ev_xn[tt] = pr.v(lambda e, xap=xap, tt=tt, sl=sl: e.tensor_scalar(
                AFp(o_xn + 1024 * tt, 1024), xap,
                st_n[:, sl:sl + 1], st_r[:, sl:sl + 1], ALU.mult, ALU.add),
                [e10] + ([ev_xn_guard] if ev_xn_guard else []))
            xuse.append(ev_xn[tt])
        ev_hT = {}
        htst = []
        last_trans = None
        for f in range(FC):
            par = f % 2
            RR = RO[par]
            evt = None
            wb = RR.wbegin()
            for tt in range(GT):
                wts = [ev_xn[tt]] + (wb if tt == 0 else [])
                evt = pr.t(lambda e, f=f, tt=tt, par=par: e.transpose(
                    psO[:, 512 * par + 128 * tt:512 * par + 128 * tt + 128],
                    AFp(o_xn + 1024 * tt + 128 * f, 128), ident_f[:, :]), wts)
            RR.wend(evt)
            last_trans = evt
            dst = hT_dst_fn(f)
            ev_hT[f] = pr.v(lambda e, f=f, par=par, dst=dst: e.tensor_scalar(
                dst, psO[:, 512 * par:512 * par + 512],
                caps[:, capcol + f:capcol + f + 1],
                modT[:, sacol + f:sacol + f + 1], ALU.mult, ALU.add),
                RR.r() + [evt] + list(ev_hT_wait_extra))
            RR.radd(ev_hT[f])
            if track_max is not None:
                em = pr.v(lambda e, dst=dst: e.reduce_max(
                    st_x[:, 0:1], dst, axis=AX.X, apply_absolute_value=True),
                    [ev_hT[f]], small=True)
                pr.v(lambda e, track_max=track_max: e.tensor_max(
                    track_max[:, 0:1], track_max[:, 0:1], st_x[:, 0:1]), [em],
                    small=True)
            if ht_dma:
                col0 = b * T + 512 * g
                htst.append(ST(d_ht[128 * f:128 * f + 128, col0:col0 + 512],
                               dst, [ev_hT[f]], slot="SH"))
        if x1_dma:
            for tt in range(GT):
                row0 = b * T + 512 * g + 128 * tt
                xap = AFp(o_xsb + 4096 * (g % 2) + 1024 * tt, 1024)
                ev = ST(d_x1[row0:row0 + 128, :], xap, [ev_xn[tt]], slot="SX")
                xuse.append(ev)
        return ev_hT, xuse, last_trans, htst

    for b in (range(NB) if "p1" in phases else ()):
        ev_vainit = pr.v(lambda e: e.memset(AB(o_va, 16640), 1.0))
        ev_xuse = {}
        ev_qkv_T_done = None
        ev_outp_T_done = None
        ev_oevac = {}
        ev_xn_guard = None
        prev_ht_guard = []
        def _x_loads(g):
            gp = g % 2
            out = []
            for tt in range(GT):
                row0 = b * T + 512 * g + 128 * tt
                w = ev_xuse.get((gp, tt), [])
                ev = LD(AFp(o_xsb + 4096 * gp + 1024 * tt, 1024),
                        d_x[row0:row0 + 128, :], w,
                        slot="LA" if gp == 0 else "LB")
                out.append([ev])
            return out

        pending_x = None
        for g in range(NG):
            gpar = g % 2
            exr = pending_x if pending_x is not None else _x_loads(g)
            pending_x = _x_loads(g + 1) if g + 1 < NG else None
            ev_hT, xuse1, ev_xn_guard, _ = ln_group(
                b, g, capcol=16 * b, sacol=32 * b,
                ev_x_ready=exr, ev_xn_guard=ev_xn_guard,
                hT_dst_fn=lambda f: AB(o_hT + 512 * f, 512),
                ev_hT_wait_extra=[ev_qkv_T_done] if ev_qkv_T_done else [],
                track_max=None, ht_dma=False, x1_dma=False)
            ev_wld = {}
            ev_blk_T = {}

            def _qkv_ld(blk, w):
                pi = blk % 2
                return LD(
                    AB(o_wqb + 4096 * pi, 4096),
                    d_wqkvs[:, 512 * blk:512 * blk + 512]
                    .rearrange("(f p) j -> p f j", p=128), w,
                    slot="LW0" if pi == 0 else "LW1")

            for blk in range(2):
                ev_wld[blk] = _qkv_ld(
                    blk, [ev_outp_T_done] if ev_outp_T_done else [])
            ev_qkv_ev = []
            for blk in range(6):
                pi = blk % 2
                wq = AB(o_wqb + 4096 * pi, 4096)
                if blk < 4:
                    for fosub in range(4):
                        fo = 4 * blk + fosub
                        bank = fo % 2
                        psreg = psS[:, 512 * bank:512 * bank + 512]
                        wb = RS[bank].wbegin()
                        evm = None
                        for f in range(FC):
                            wts = [ev_wld[blk], ev_hT[f]] + (wb if f == 0 else [])
                            evm = pr.t(lambda e, f=f, fosub=fosub, psreg=psreg, wq=wq:
                                       e.matmul(psreg,
                                                wq[:, 512 * f + 128 * fosub:
                                                   512 * f + 128 * fosub + 128],
                                                AB(o_hT + 512 * f, 512),
                                                start=(f == 0), stop=(f == FC - 1)),
                                       wts)
                        RS[bank].wend(evm)
                        if fo < 8:
                            eve = pr.s(lambda e, fo=fo, psreg=psreg: e.copy(
                                AB(o_qT + 512 * fo, 512), psreg),
                                RS[bank].r() + [evm])
                        else:
                            kc = fo - 8
                            eve = pr.s(lambda e, kc=kc, g=g, psreg=psreg: e.mul(
                                AB(o_kT + 2048 * kc + 512 * g, 512), psreg, 0.125),
                                RS[bank].r() + [evm])
                        RS[bank].radd(eve)
                        ev_qkv_ev.append(eve)
                        ev_blk_T[blk] = evm
                else:
                    vb = blk - 4
                    for ts in range(GT):
                        bank = 2 + ts % 2
                        psreg = psS[:, 512 * bank:512 * bank + 512]
                        wb = RS[bank].wbegin()
                        evm = None
                        for f in range(FC):
                            wts = [ev_wld[blk], ev_hT[f]] + (wb if f == 0 else [])
                            evm = pr.t(lambda e, f=f, ts=ts, psreg=psreg, wq=wq:
                                       e.matmul(psreg,
                                                AB(o_hT + 512 * f + 128 * ts, 128),
                                                wq[:, 512 * f:512 * f + 512],
                                                start=(f == 0), stop=(f == FC - 1)),
                                       wts)
                        RS[bank].wend(evm)
                        kt = 4 * g + ts
                        eve = pr.s(lambda e, kt=kt, vb=vb, psreg=psreg: e.copy(
                            AB(o_va + 1040 * kt, 1040)
                            .rearrange("p (h d) -> p h d", d=65)
                            [:, 8 * vb:8 * vb + 8, 0:64],
                            psreg.rearrange("p (h d) -> p h d", d=64)),
                            RS[bank].r() + [evm, ev_vainit])
                        RS[bank].radd(eve)
                        ev_qkv_ev.append(eve)
                        ev_blk_T[blk] = evm
                if blk + 2 < 6:
                    ev_wld[blk + 2] = _qkv_ld(blk + 2, [ev_blk_T[blk]])
            ev_qkv_T_done = ev_blk_T[5]
            ev_qkv_evac = ev_qkv_ev[-1]
            qkv_all = list(ev_qkv_ev)

            nkt = 4 * g + 4
            ev_pv_last = None
            pend_z = None
            prev_pv = None

            def emit_z(zA, zB, z_pv_last):
                for half, h in ((0, zA), (1, zB)):
                    kcc, off = h // 2, 64 * (h % 2)
                    oreg = psO[:, 512 * half:512 * half + 512]
                    ev_ln = pr.s(lambda e, oreg=oreg, half=half: e.activation(
                        lnzb[64:65, 512 * half:512 * half + 512], oreg[64:65, :],
                        AF.Ln), RO[half].r() + [z_pv_last], small=True)
                    RO[half].radd(ev_ln)
                    RZ = RX if half == 0 else RY
                    psZ = psX if half == 0 else psY
                    evzb = pr.t(lambda e, psZ=psZ, half=half: e.matmul(
                        psZ[0:64, :], ones_f[64:65, 0:64],
                        lnzb[64:65, 512 * half:512 * half + 512],
                        start=True, stop=True), [ev_ln] + RZ.wbegin())
                    RZ.wend(evzb)
                    evzc = pr.s(lambda e, psZ=psZ, half=half: e.activation(
                        AB(o_rzb, 1024, 0, 64)[0:64, 512 * half:512 * half + 512],
                        psZ[0:64, :], AF.Exp, scale=-1.0), RZ.r() + [evzb])
                    RZ.radd(evzc)
                    ev_oevac[h] = pr.v(lambda e, kcc=kcc, off=off, oreg=oreg,
                                       half=half: e.tensor_mul(
                        AB(o_od + 512 * kcc, 512, off, off + 64),
                        oreg[0:64, :],
                        AB(o_rzb, 1024, 0, 64)[0:64, 512 * half:512 * half + 512]),
                        [evzc])
                    RO[half].radd(ev_oevac[h])

            for hp in range(H // 2):
                hA, hB = 2 * hp, 2 * hp + 1
                kc = hp
                ev_exp = {}
                ev_pv = {}

                def emit_pv(kt):
                    nonlocal ev_pv_last
                    spar = kt % 2
                    expA, expB = ev_exp[kt]
                    for half, h in ((0, hA), (1, hB)):
                        wts = [expA if half == 0 else expB]
                        if kt == 0:
                            if h - 2 in ev_oevac:
                                wts += [ev_oevac[h - 2]]
                            wts += RO[half].wbegin()
                        evp = pr.t(lambda e, kt=kt, h=h, half=half, spar=spar:
                                   e.matmul(psO[0:65, 512 * half:512 * half + 512],
                                            AB(o_va + 1040 * kt + 65 * h, 65),
                                            AB(o_pb + 1024 * spar + 512 * half, 512),
                                            start=(kt == 0), stop=(kt == nkt - 1)),
                                   wts)
                        if half == 1:
                            ev_pv[kt] = evp
                        if kt == nkt - 1:
                            RO[half].wend(evp)
                            ev_pv_last = evp

                for kt in range(nkt):
                    spar = kt % 2
                    bA, bB = 2 * spar, 2 * spar + 1
                    wb = RS[bA].wbegin() + RS[bB].wbegin()
                    wts = list(qkv_all) + wb
                    ev_scA = pr.t(lambda e, kt=kt, kc=kc, spar=spar: e.matmul(
                        psS[:, 1024 * spar:1024 * spar + 512],
                        AB(o_kT + 2048 * kc + 128 * kt, 128, 0, 64),
                        AB(o_qT + 512 * kc, 512, 0, 64),
                        start=True, stop=True), wts)
                    ev_sc = pr.t(lambda e, kt=kt, kc=kc, spar=spar: e.matmul(
                        psS[:, 1024 * spar + 512:1024 * spar + 1024],
                        AB(o_kT + 2048 * kc + 128 * kt, 128, 64, 128),
                        AB(o_qT + 512 * kc, 512, 64, 128),
                        start=True, stop=True), [])
                    RS[bA].wend(ev_sc)
                    RS[bB].wend(ev_sc)
                    ev_mh = {0: [], 1: []}
                    r = kt - 4 * g
                    if r >= 0:
                        for half in range(2):
                            if r > 0:
                                ev_mh[half].append(pr.v(
                                    lambda e, spar=spar, half=half, r=r:
                                    e.tensor_scalar_add(
                                        psS[:, 1024 * spar + 512 * half:
                                            1024 * spar + 512 * half + 128 * r],
                                        psS[:, 1024 * spar + 512 * half:
                                            1024 * spar + 512 * half + 128 * r],
                                        -1.0e9), [ev_sc]))
                            ev_mh[half].append(pr.v(
                                lambda e, spar=spar, half=half, r=r:
                                e.tensor_add(
                                    psS[:, 1024 * spar + 512 * half + 128 * r:
                                        1024 * spar + 512 * half + 128 * r + 128],
                                    psS[:, 1024 * spar + 512 * half + 128 * r:
                                        1024 * spar + 512 * half + 128 * r + 128],
                                    cmask[:, :]), [ev_sc]))
                    for half, bk in ((0, bA), (1, bB)):
                        for evm in ev_mh[half]:
                            RS[bk].radd(evm)
                            RS[bk].wend(evm)
                    if kt >= 2:
                        emit_pv(kt - 2)
                    guard = []
                    if kt >= 2:
                        guard = [ev_pv[kt - 2]]
                    elif prev_pv is not None:
                        guard = [prev_pv[nkt - 2 + kt]]
                    ev_eA = pr.s(lambda e, spar=spar: e.activation(
                        AB(o_pb + 1024 * spar, 512),
                        psS[:, 1024 * spar:1024 * spar + 512], AF.Exp),
                        [ev_sc] + ev_mh[0] + guard)
                    ev_eB = pr.s(lambda e, spar=spar: e.activation(
                        AB(o_pb + 1024 * spar + 512, 512),
                        psS[:, 1024 * spar + 512:1024 * spar + 1024], AF.Exp),
                        ev_mh[1])
                    ev_exp[kt] = (ev_eA, ev_eB)
                    RS[bA].radd(ev_eA)
                    RS[bB].radd(ev_eB)
                    if kt == 1 and pend_z is not None:
                        emit_z(*pend_z)
                        pend_z = None
                emit_pv(nkt - 2)
                emit_pv(nkt - 1)
                prev_pv = ev_pv
                pend_z = (hA, hB, ev_pv_last)
            emit_z(*pend_z)
            pend_z = None

            ev_wo = {}
            for hb in range(2):
                ev_wo[hb] = LD(
                    AB(o_wqb + 4096 * hb, 4096),
                    d_wouts[:, 512 * hb:512 * hb + 512]
                    .rearrange("(f p) j -> p f j", p=128),
                    [ev_qkv_T_done], slot="LW0" if hb == 0 else "LW1")
            ev_x1 = {tt: [] for tt in range(GT)}
            for tt in range(GT):
                for hb in range(2):
                    bank = 2 * ((2 * tt + hb) % 2)
                    banks = (bank, bank)
                    psreg = psS[:, 512 * bank:512 * bank + 512]
                    wb = RS[bank].wbegin()
                    evm = None
                    for dc in range(FC):
                        wts = [ev_oevac[2 * dc], ev_oevac[2 * dc + 1],
                               ev_wo[hb]] + (wb if dc == 0 else [])
                        evm = pr.t(lambda e, dc=dc, tt=tt, hb=hb, psreg=psreg:
                                   e.matmul(psreg,
                                            AB(o_od + 512 * dc + 128 * tt, 128),
                                            AB(o_wqb + 4096 * hb + 512 * dc, 512),
                                            start=(dc == 0), stop=(dc == FC - 1)),
                                   wts)
                    RS[bank].wend(evm)
                    xap = AFp(o_xsb + 4096 * gpar + 1024 * tt + 512 * hb, 512)
                    ev_t = pr.v(lambda e, psreg=psreg, b=b, hb=hb: e.tensor_mul(
                        AFp(o_tev, 512), psreg,
                        gabc[:, C * b + 512 * hb:C * b + 512 * hb + 512]),
                        RS[bank].r() + [evm])
                    RS[bank].radd(ev_t)
                    ev_a = pr.v(lambda e, xap=xap: e.tensor_add(
                        xap, xap, AFp(o_tev, 512)), [ev_t])
                    ev_x1[tt].append(ev_a)
            ev_outp_T_done = evm

            ev_hT2, xuse2, ev_xn_guard, ev_ht_guard = ln_group(
                b, g, capcol=16 * b + 8, sacol=32 * b + 16,
                ev_x_ready=[ev_x1[tt] for tt in range(GT)],
                ev_xn_guard=ev_xn_guard,
                hT_dst_fn=lambda f: AB(o_ht2 + 512 * f, 512),
                ev_hT_wait_extra=list(prev_ht_guard),
                track_max=hmax1, ht_dma=True, x1_dma=True)
            prev_ht_guard = ev_ht_guard
            ev_xuse = dict(ev_xuse)
            for tt in range(GT):
                ev_xuse[(gpar, tt)] = xuse1 + xuse2
        pr.rendezvous()

    def allreduce_max(local, cci, cco, scol):
        ev1 = ST(cci.ap(), local[:, 0:1])
        evc = pr.cc(lambda e: e.collective_compute(
            "AllReduce", ALU.max, replica_groups=[list(range(8))],
            ins=[cci.ap()], outs=[cco.ap()]), [ev1])
        ev2 = pr.dg(lambda e: e.dma_start(out=gmt[:, 0:1], in_=cco.ap()), [evc])
        evt = pr.t(lambda e: e.transpose(psX[0:1, 0:128], gmt[:, 0:1], ident_f[:, :]),
                   [ev2] + RX.wbegin())
        RX.wend(evt)
        evr = pr.v(lambda e, scol=scol: e.reduce_max(
            scal[0:1, scol + 2:scol + 3], psX[0:1, 0:128], axis=AX.X),
            RX.r() + [evt], small=True)
        RX.radd(evr)
        evs = pr.v(lambda e, scol=scol: e.tensor_scalar(
            scal[0:1, scol:scol + 1], scal[0:1, scol + 2:scol + 3],
            sigx[0:1, scol // 4:scol // 4 + 1], 1.0 / 127.0, ALU.mult, ALU.mult),
            [evr], small=True)
        evrs = pr.v(lambda e, scol=scol: e.reciprocal(
            scal[0:1, scol + 1:scol + 2], scal[0:1, scol:scol + 1]), [evs],
            small=True)
        return evs, evrs

    if "p2" not in phases:
        allw = [(s, pr.cnt[s]) for s in DMA_SEMS]
        pr.op("Y", None, allw)
        pr.op("G", None, allw)
        with nc.Block() as blk:
            blk.tensor(lambda e: pr.replay("T", e, sems))
            blk.vector(lambda e: pr.replay("V", e, sems))
            blk.scalar(lambda e: pr.replay("S", e, sems))
            blk.gpsimd(lambda e: pr.replay("G", e, sems))
            blk.sync(lambda e: pr.replay("Y", e, sems))
        ctx.close()
        return nc
    evs_x, evrs_x = allreduce_max(hmax1, cc1i, cc1o, 0)
    ev_bc1 = pr.t(lambda e: e.matmul(psY[:, 0:2], ones_f[0:1, :], scal[0:1, 0:2],
                                     start=True, stop=True),
                  [evs_x, evrs_x] + RY.wbegin())
    RY.wend(ev_bc1)
    ev_sxbc = pr.v(lambda e: e.tensor_copy(sxbc[:, 0:2], psY[:, 0:2]),
                   RY.r() + [ev_bc1], small=True)
    RY.radd(ev_sxbc)
    ev_sxg = pr.v(lambda e: e.tensor_scalar(sxswg[:, :], swg[:, :], sxbc[:, 0:1],
                                            None, ALU.mult, ALU.bypass), [ev_sxbc],
                  small=True)
    ev_sxu = pr.v(lambda e: e.tensor_scalar(sxswu[:, :], swu[:, :], sxbc[:, 0:1],
                                            None, ALU.mult, ALU.bypass), [ev_sxbc],
                  small=True)

    ev_wgld = LD(AB(o_wgT, 22528), d_wgs.rearrange("(f p) i -> p f i", p=128))
    ev_wuld = pr.dg(lambda e: e.dma_start(
        out=AB(o_wuT, 22528),
        in_=d_wus.rearrange("(f p) i -> p f i", p=128)))
    ev_ht_used = [None, None]
    ev_hm_st = {}
    ev_htld = {}
    ev_gu_last = {}

    def _ht_ld(gg):
        hp = gg % 2
        w = [ev_ht_used[hp]] if ev_ht_used[hp] else []
        return LD(AB(o_htb + 4096 * hp, 4096),
                  d_ht[:, 512 * gg:512 * gg + 512]
                  .rearrange("(f p) t -> p f t", p=128),
                  w, slot="LA" if hp == 0 else "LB")

    ev_htld[0] = _ht_ld(0)
    for gg in range(NB * NG):
        col0 = 512 * gg
        hpar = gg % 2
        if gg + 1 < NB * NG:
            ev_htld[gg + 1] = _ht_ld(gg + 1)
        ev_xq = {}
        for f in range(FC):
            gmm = ([ev_gu_last[gg - 1]] if f == 0 and gg - 1 in ev_gu_last
                   else [])
            e1 = pr.v(lambda e, f=f, hpar=hpar: e.tensor_scalar(
                AFp(o_tev, 512), AB(o_htb + 4096 * hpar + 512 * f, 512),
                sxbc[:, 1:2], QLO, ALU.mult, ALU.max),
                [ev_htld[gg], ev_sxbc] + gmm)
            e1b = pr.v(lambda e: e.tensor_scalar_min(
                sq_t[:, 1024:1536], AFp(o_tev, 512), QHI), [e1])
            e2 = pr.v(lambda e: e.tensor_copy(xqi[:, :], sq_t[:, 1024:1536]), [e1b])
            ev_xq[f] = pr.v(lambda e, f=f: e.tensor_copy(
                AB(o_xq + 512 * f, 512), xqi[:, :]), [e2])
        ev_ht_used[hpar] = ev_xq[FC - 1]
        for icx in range(IC):
            gbank = (icx % 2) * 2
            ubank = gbank + 1
            gps = psS[:, 512 * gbank:512 * gbank + 512]
            ups = psS[:, 512 * ubank:512 * ubank + 512]
            wbg = RS[gbank].wbegin()
            wbu = RS[ubank].wbegin()
            evg = evu = None
            for f in range(FC):
                wts = [ev_xq[f], ev_wgld] + (wbg if f == 0 else [])
                evg = pr.t(lambda e, f=f, icx=icx, gps=gps: e.matmul(
                    gps, AB(o_wgT + 2816 * f + 128 * icx, 128),
                    AB(o_xq + 512 * f, 512),
                    start=(f == 0), stop=(f == FC - 1)), wts)
            for f in range(FC):
                wts = [ev_xq[f], ev_wuld] + (wbu if f == 0 else [])
                evu = pr.t(lambda e, f=f, icx=icx, ups=ups: e.matmul(
                    ups, AB(o_wuT + 2816 * f + 128 * icx, 128),
                    AB(o_xq + 512 * f, 512),
                    start=(f == 0), stop=(f == FC - 1)), wts)
            RS[gbank].wend(evg)
            RS[ubank].wend(evu)
            ev_sil = pr.s(lambda e, icx=icx, gps=gps: e.activation(
                AB(o_sil, 512), gps, AF.Silu, scale=sxswg[:, icx:icx + 1]),
                RS[gbank].r() + [evg, ev_sxg])
            RS[gbank].radd(ev_sil)
            hmkey = gg * IC + icx
            hmslot = hmkey % 3
            ev_hm = pr.v(lambda e, icx=icx, ups=ups, hmslot=hmslot: e.scalar_tensor_tensor(
                AB(o_hm + 512 * hmslot, 512), ups, sxswu[:, icx:icx + 1],
                AB(o_sil, 512), ALU.mult, ALU.mult),
                RS[ubank].r() + [evu, ev_sil, ev_sxu] +
                ([ev_hm_st[hmkey - 3]] if hmkey - 3 in ev_hm_st else []))
            RS[ubank].radd(ev_hm)
            em = pr.v(lambda e, hmslot=hmslot: e.reduce_max(
                st_x[:, 1:2], AB(o_hm + 512 * hmslot, 512), axis=AX.X,
                apply_absolute_value=True), [ev_hm], small=True)
            pr.v(lambda e: e.tensor_max(hmax2[:, 0:1], hmax2[:, 0:1], st_x[:, 1:2]),
                 [em], small=True)
            ev_hm_st[hmkey] = ST(
                d_hm[128 * icx:128 * icx + 128, col0:col0 + 512],
                AB(o_hm + 512 * hmslot, 512), [ev_hm],
                slot=("SM0", "SM1", "SM2")[hmslot])
        ev_gu_last[gg] = evu

    pr.rendezvous()

    evs_d, evrs_d = allreduce_max(hmax2, cc2i, cc2o, 4)
    ev_bc2 = pr.t(lambda e: e.matmul(psY[:, 0:2], ones_f[0:1, :], scal[0:1, 4:6],
                                     start=True, stop=True),
                  [evs_d, evrs_d] + RY.wbegin())
    RY.wend(ev_bc2)
    ev_sdbc = pr.v(lambda e: e.tensor_copy(sdbc[:, 0:2], psY[:, 0:2]),
                   RY.r() + [ev_bc2], small=True)
    RY.radd(ev_sdbc)

    ev_wdldA = LD(AB(o_wdT, 11264),
                  d_wds.rearrange("(f p) h -> p f h", p=128)[:, 0:11])
    ev_wdldB = pr.dg(lambda e: e.dma_start(
        out=AB(o_wdT + 11264, 11264),
        in_=d_wds.rearrange("(f p) h -> p f h", p=128)[:, 11:22]))
    ev_outst_prev = {}
    ev_hmld = {}
    ev_hm_rd = {}
    ev_mm_last = {}

    def _hm_ld(gg):
        par = gg % 2
        w = [ev_hm_rd[gg - 2]] if gg - 2 in ev_hm_rd else []
        if par == 1:
            return pr.dg(lambda e, gg=gg: e.dma_start(
                out=AB(o_hmg + 11264, 11264),
                in_=d_hm[:, 512 * gg:512 * gg + 512]
                .rearrange("(f p) t -> p f t", p=128)), w)
        return LD(AB(o_hmg, 11264),
                  d_hm[:, 512 * gg:512 * gg + 512]
                  .rearrange("(f p) t -> p f t", p=128),
                  w, slot="LP0")

    ev_hmld[0] = _hm_ld(0)
    for gg in range(NB * NG):
        b = gg // NG
        par = gg % 2
        col0 = 512 * gg
        if gg + 1 < NB * NG:
            ev_hmld[gg + 1] = _hm_ld(gg + 1)
        ev_h2 = {}
        for icx in range(IC):
            gmm = ([ev_mm_last[gg - 1]] if icx == 0 and gg - 1 in ev_mm_last
                   else [])
            e1 = pr.v(lambda e, icx=icx, par=par: e.tensor_scalar(
                AFp(o_tev, 512), AB(o_hmg + 11264 * par + 512 * icx, 512),
                sdbc[:, 1:2], QLO, ALU.mult, ALU.max),
                [ev_hmld[gg], ev_sdbc] + gmm)
            e1b = pr.v(lambda e: e.tensor_scalar_min(
                sq_t[:, 1024:1536], AFp(o_tev, 512), QHI), [e1])
            e2 = pr.v(lambda e: e.tensor_copy(xqi[:, :], sq_t[:, 1024:1536]), [e1b])
            ev_h2[icx] = pr.v(lambda e, icx=icx: e.tensor_copy(
                AB(o_h2i + 512 * icx, 512), xqi[:, :]), [e2])
        ev_hm_rd[gg] = ev_h2[IC - 1]
        for tt in range(GT):
            gt = gg * GT + tt
            row0 = col0 + 128 * tt
            x1par = gt % 2
            ev_x1ld = LD(AFp(o_x1b + 1024 * x1par, 1024), d_x1[row0:row0 + 128, :],
                         [ev_outst_prev.get(gt - 2)],
                         slot="LW0" if x1par == 0 else "LW1")
            evos = []
            for hb in range(2):
                bank = 2 * tt + hb
                bank = bank % 4
                psreg = psS[:, 512 * bank:512 * bank + 512]
                wb = RS[bank].wbegin()
                evm = None
                for icx in range(IC):
                    wts = [ev_h2[icx],
                           ev_wdldA if icx < 11 else ev_wdldB] + \
                        (wb if icx == 0 else [])
                    evm = pr.t(lambda e, icx=icx, tt=tt, hb=hb, psreg=psreg: e.matmul(
                        psreg, AB(o_h2i + 512 * icx + 128 * tt, 128),
                        AB(o_wdT + 1024 * icx + 512 * hb, 512),
                        start=(icx == 0), stop=(icx == IC - 1)), wts)
                RS[bank].wend(evm)
                ev_t = pr.v(lambda e, psreg=psreg, b=b, hb=hb: e.scalar_tensor_tensor(
                    AFp(o_tev, 512), psreg, sdbc[:, 0:1],
                    gbbc[:, C * b + 512 * hb:C * b + 512 * hb + 512],
                    ALU.mult, ALU.mult), RS[bank].r() + [evm, ev_sdbc])
                RS[bank].radd(ev_t)
                ev_o = pr.v(lambda e, x1par=x1par, hb=hb: e.tensor_add(
                    AFp(o_outs + 1024 * x1par + 512 * hb, 512),
                    AFp(o_x1b + 1024 * x1par + 512 * hb, 512),
                    AFp(o_tev, 512)),
                    [ev_t, ev_x1ld, ev_outst_prev.get(gt - 2)])
                evos.append(ev_o)
            ev_outst_prev[gt] = ST(d_out[row0:row0 + 128, :],
                                   AFp(o_outs + 1024 * x1par, 1024), evos,
                                   slot="SO0" if x1par == 0 else "SO1")
        ev_mm_last[gg] = evm

    allw = [(s, pr.cnt[s]) for s in DMA_SEMS]
    pr.op("Y", None, allw)
    pr.op("G", None, allw)

    with nc.Block() as blk:
        blk.tensor(lambda e: pr.replay("T", e, sems))
        blk.vector(lambda e: pr.replay("V", e, sems))
        blk.scalar(lambda e: pr.replay("S", e, sems))
        blk.gpsimd(lambda e: pr.replay("G", e, sems))
        blk.sync(lambda e: pr.replay("Y", e, sems))

    ctx.close()
    return nc


_NC_CACHE = {}


def get_program(dbg=(), phases=("p0", "p1", "p2")):
    key = (tuple(sorted(dbg)), tuple(phases))
    if key not in _NC_CACHE:
        _NC_CACHE[key] = build_program(dbg, phases)
    return _NC_CACHE[key]


def kernel(L0_attn_x, L0_attn_w_qkv, L0_attn_w_out, L0_attn_head_scales,
           L0_attn_head_dirs, L0_ffn_clip_x, L0_ffn_clip_down, L0_ffn_w_gate,
           L0_ffn_w_up, L0_ffn_w_down, L0_c, L0_wmod, _dbg=(), _ret_raw=False,
           _phases=("p0", "p1", "p2"), _trace=False):
    nc = get_program(_dbg, _phases)
    x = np.ascontiguousarray(np.asarray(L0_attn_x, dtype=np.float32))
    wqkv = np.ascontiguousarray(np.asarray(L0_attn_w_qkv, dtype=np.float32))
    wout = np.ascontiguousarray(np.asarray(L0_attn_w_out, dtype=np.float32))
    hs = np.ascontiguousarray(np.asarray(L0_attn_head_scales,
                                         dtype=np.float32).reshape(H, 1))
    dirs = np.ascontiguousarray(np.asarray(L0_attn_head_dirs, dtype=np.float32))
    clipx = np.asarray(L0_ffn_clip_x, dtype=np.float32).reshape(1, 1)
    clipd = np.asarray(L0_ffn_clip_down, dtype=np.float32).reshape(1, 1)
    wg = np.ascontiguousarray(np.asarray(L0_ffn_w_gate, dtype=np.float32))
    wu = np.ascontiguousarray(np.asarray(L0_ffn_w_up, dtype=np.float32))
    wd = np.ascontiguousarray(np.asarray(L0_ffn_w_down, dtype=np.float32))
    cv = np.ascontiguousarray(np.asarray(L0_c, dtype=np.float32))
    wmod = np.ascontiguousarray(np.asarray(L0_wmod, dtype=np.float32))

    in_maps = []
    for i in range(8):
        in_maps.append({
            "x": np.ascontiguousarray(x[2 * i:2 * i + 2].reshape(NB * T, C)),
            "wqkv": wqkv, "wout": wout, "hs": hs, "dirs": dirs,
            "clipx": clipx, "clipd": clipd,
            "wg": wg, "wu": wu, "wd": wd,
            "cvec": np.ascontiguousarray(cv[2 * i:2 * i + 2]),
            "wmod": wmod,
        })
    res = run_bass_kernel_spmd(nc, in_maps, core_ids=list(range(8)),
                               trace=_trace)
    if _ret_raw:
        return res
    out = np.stack([res.results[i]["out"].reshape(NB, T, C) for i in range(8)])
    return out.reshape(B, T, C)


if __name__ == "__main__":
    nc = build_program()
    n = sum(1 for _ in nc.all_instructions())
    print("built ok, instructions:", n)
